# version 03 (final)
# baseline (speedup 1.0000x reference)
import os
import numpy as np
import ml_dtypes
import concourse.bass as bass
import concourse.mybir as mybir
from concourse.bass_utils import run_bass_kernel_spmd
from concourse.expressions import smax
from concourse.expressions_rust import update_runtime_value
from contextlib import ExitStack

F32 = mybir.dt.float32
BF16 = mybir.dt.bfloat16
AF = mybir.ActivationFunctionType
ALU = mybir.AluOpType
AX = mybir.AxisListType
ET = mybir.EngineType

D = 2048
NE = 18
FF = 1024
H, HD = 16, 128
SUP = 512
NEG = -1e30
LNEPS = 1e-5
RMSEPS = 1e-6
SCALE = float(1.0 / np.sqrt(128.0))
M_BLK = 16
CI = 64
BISECT = 42


def build_program(T=4096, phases=("p1", "p2", "p3", "p4"), ne_act=NE):
    KDROP = set(os.environ.get("KDROP", "").split(","))
    NS = T // SUP
    nc = bass.Bass("TRN2", target_bir_lowering=False, debug=False)

    inp = {}
    def I(name, shape):
        inp[name] = nc.dram_tensor(name, list(shape), F32, kind="ExternalInput").ap()
        return inp[name]

    I("x", (T, D))
    I("w_qc", (D, 512)); I("w_qup", (512, D))
    I("kvc_w", (D, HD)); I("kvc_wz", (D, HD)); I("kvc_bias", (M_BLK, HD))
    I("k_proj_w", (D, HD)); I("v_proj_w", (D, HD))
    I("idx_c_w", (D, CI)); I("idx_c_wz", (D, CI)); I("idx_c_bias", (M_BLK, CI))
    I("w_dq", (D, CI)); I("w_iuq", (CI, 4 * CI)); I("w_w", (D, 4))
    I("q_norm_w", (HD,)); I("k_norm_w", (HD,))
    I("group_w", (4, 512, 512)); I("final_w", (D, D))
    I("Wq", (D, D)); I("bq", (D,)); I("Wk", (D, HD)); I("bk", (HD,))
    I("Wv", (D, HD)); I("bv", (HD,)); I("Wo", (D, D)); I("bo", (D,))
    I("wa", (D, 16))
    I("rg", (16, D, FF)); I("rgb", (16, FF)); I("rw1", (16, D, FF)); I("rb1", (16, FF))
    I("rw2", (16, FF, D)); I("rb2", (16, D))
    I("sg", (2, D, FF)); I("sgb", (2, FF)); I("sw1", (2, D, FF)); I("sb1", (2, FF))
    I("sw2", (2, FF, D)); I("sb2", (2, D))
    I("ident", (128, 128))
    I("cosT", (32, T)); I("sinT", (32, T))
    I("cmask", (T, 256)); I("cbias", (T, 256))
    I("swm", (512, 256)); I("swm0", (512, 256))

    y_out = nc.dram_tensor("y", [T, D], F32, kind="ExternalOutput").ap()

    def Scr(name, shape, dtype):
        return nc.dram_tensor(name, list(shape), dtype, kind="Internal").ap()

    xs = {1: Scr("xs1", (T, D), F32), 2: Scr("xs2", (T, D), F32), 3: Scr("xs3", (T, D), F32)}
    WGs = Scr("WGs", (NE, 2, 128, 16, 512), BF16)
    WW1s = Scr("WW1s", (NE, 2, 128, 16, 512), BF16)
    WW2s = Scr("WW2s", (NE, 2, 128, 8, 1024), BF16)
    WFINs = Scr("WFINs", (16, 128, D), BF16)
    WQs = Scr("WQs", (16, 128, 16, 128), BF16)
    WOs = Scr("WOs", (4, 16, 128, 512), BF16)
    qTs = Scr("qTs", (H, 128, T), BF16)
    qiTs = Scr("qiTs", (4, 128, T), BF16)
    WPA = Scr("WPA", (128, 4, 8192), BF16)
    WGRPs = Scr("WGRPs", (128, 4, 4, 512), BF16)
    ATs = Scr("ATs", (128, 16 * T), BF16)
    dumS = Scr("dumS", (128, 1), F32)

    need_ffn = ("p2" in phases) or ("p4" in phases) or ("conv" in phases)

    import re as _re
    _orig_alloc_regs = nc.alloc_registers
    _reg_cache = {}

    def _cached_alloc_registers(name=None, engines=None):
        key = (tuple(engines) if engines is not None else None,
               _re.sub(r"\d+", "", name or ""))
        if key not in _reg_cache:
            _reg_cache[key] = _orig_alloc_regs(name, engines=engines)
        return _reg_cache[key]

    nc.alloc_registers = _cached_alloc_registers

    from contextlib import contextmanager as _ctxm

    def free_snap(s_):
        try:
            for h in s_.val.handles:
                nc.free_register(h)
        except AttributeError:
            pass

    st = ExitStack()
    with st:
        en = st.enter_context

        sbc = en(nc.sbuf_tensor("sbc", [128, 8], F32))
        identb = en(nc.sbuf_tensor("identb", [128, 128], BF16))
        identf = en(nc.sbuf_tensor("identf", [128, 128], F32))
        onesrow = en(nc.sbuf_tensor("onesrow", [1, 128], F32))
        onesrowb = en(nc.sbuf_tensor("onesrowb", [1, 128], BF16))
        onescol = en(nc.sbuf_tensor("onescol", [128, 1], F32))
        onescolb = en(nc.sbuf_tensor("onescolb", [128, 1], BF16))
        PB1 = en(nc.sbuf_tensor("PB1", [128, 24576], BF16))
        PB2 = en(nc.sbuf_tensor("PB2", [128, 16384], BF16))
        PB3 = en(nc.sbuf_tensor("PB3", [128, 10240], BF16))
        PF1 = en(nc.sbuf_tensor("PF1", [128, 8192], F32))
        PF2 = en(nc.sbuf_tensor("PF2", [128, 6144], F32))
        PE1 = en(nc.sbuf_tensor("PE1", [128, 9216], BF16))
        SM = en(nc.sbuf_tensor("SM", [128, 1024], F32))
        BIAS = en(nc.sbuf_tensor("BIAS", [128, 512], F32))
        B2CAT = en(nc.sbuf_tensor("B2CAT", [32, D], F32))
        BOROW = en(nc.sbuf_tensor("BOROW", [1, D], F32))
        COMBT = en(nc.sbuf_tensor("COMBT", [32, SUP], F32))

        ps = [en(nc.psum_tensor(f"ps{i}", [128, 512], F32)) for i in range(6)]
        pst = en(nc.psum_tensor("pst", [128, 512], BF16))
        psl = en(nc.psum_tensor("psl", [128, 512], F32))

        sem = {}
        for nm in ["dS0", "dS1", "dG0", "dG1", "dST0", "dST1", "vC", "sC", "tM", "tU", "sY", "sEV", "sBC", "vS", "sS"]:
            sem[nm] = en(nc.semaphore(nm))

        prog = {"sync": [], "vector": [], "scalar": [], "tensor": [], "gpsimd": []}
        RT = {}
        Q = {"dS": 0, "dG": 0, "dST": 0}
        ECNT = {"vS": 0, "sS": 0, "vCc": 0, "sCc": 0, "tMc": 0}

        def urv(x, mx):
            return x if isinstance(x, int) else update_runtime_value(x, max_val=mx)

        def wregr(eng, key):
            if key not in RT:
                RT[key] = eng.alloc_register(key)
            return RT[key]

        def wrg(eng, rt, sname, plus_reg, jmul, off, clamp=None):
            if jmul is None and plus_reg is None:
                val = off if clamp is None else max(off, clamp)
                eng.wait_ge(sem[sname], max(val, 0))
                return
            if jmul is not None:
                jv, mul = jmul
                eng.reg_mov(rt, jv)
                eng.reg_mul(rt, rt, mul)
                eng.reg_add(rt, rt, off)
            else:
                eng.reg_mov(rt, off)
            if plus_reg is not None:
                eng.reg_alu(rt, rt, plus_reg, op=ALU.max if False else ALU.add)
            if clamp is not None:
                eng.reg_alu(rt, rt, clamp, op=ALU.max)
            eng.wait_ge(sem[sname], rt)


        class Chain:
            def __init__(self, eng, nm):
                self.eng = eng
                self.nm = nm
                self.last = None

            def do(self, emit, inc=None):
                if self.last is not None:
                    self.eng.wait_ge(*self.last)
                ins = emit()
                if inc is not None:
                    ins.then_inc(inc[0], 1)
                    self.last = (inc[0], inc[1])
                else:
                    ECNT[self.nm] += 1
                    ins.then_inc(sem[self.nm], 1)
                    self.last = (sem[self.nm], ECNT[self.nm])
                return ins

        def q_issue(qname):
            g = Q[qname]
            Q[qname] += 1
            sub = qname + str(g % 2)
            return sub, 16 * (g // 2), 16 * (g // 2 + 1)

        def wreg(eng, key, sname, j, mul, off, clamp=None):
            if key not in RT:
                RT[key] = eng.alloc_register(key)
            rt = RT[key]
            eng.reg_mov(rt, j)
            if mul != 1:
                eng.reg_mul(rt, rt, mul)
            eng.reg_add(rt, rt, off)
            if clamp is not None:
                eng.reg_alu(rt, rt, clamp, op=ALU.max)
            eng.wait_ge(sem[sname], rt)

        C = {}
        def ew(engine, fn):
            prog[engine].append(fn)
        def add(name, k=1):
            C[name] = C.get(name, 0) + k
            return C[name]
        def get(name):
            return C.get(name, 0)
        marks = {}

        def emit_startup():
            sub0, ch0, dn0 = q_issue("dS")
            def sy0(s, sub0=sub0, ch0=ch0):
                s.wait_ge(sem[sub0], ch0)
                s.dma_start(out=identf[:], in_=inp["ident"]).then_inc(sem[sub0], 16)
            ew("sync", sy0)

            def ve0(v, sub0=sub0, dn0=dn0):
                v.wait_ge(sem[sub0], dn0)
                v.memset(sbc[:, 0:1], LNEPS)
                v.memset(sbc[:, 1:2], RMSEPS)
                v.memset(sbc[:, 2:3], 1.0)
                v.memset(sbc[:, 3:4], 1e-30)
                v.memset(sbc[:, 4:5], 0.0)
                v.memset(onesrow[:], 1.0)
                v.memset(onesrowb[:], 1.0)
                v.memset(onescol[:], 1.0)
                v.memset(onescolb[:], 1.0)
                v.memset(PB1[:], 0.0)
                v.memset(PB2[:], 0.0)
                v.memset(PB3[:], 0.0)
                v.memset(PE1[:], 0.0)
                v.memset(PF1[:], 0.0)
                v.memset(PF2[:], 0.0)
                v.memset(SM[:], 0.0)
                v.tensor_copy(identb[:], identf[:]).then_inc(sem["vC"], 1)
            ew("vector", ve0)
            c_const = add("vC", 1)
            marks["const"] = c_const

            conv = []
            if need_ffn:
                for ee in range(NE):
                    g_src = inp["rg"][ee] if ee < 16 else inp["sg"][ee - 16]
                    w1_src = inp["rw1"][ee] if ee < 16 else inp["sw1"][ee - 16]
                    w2_src = inp["rw2"][ee] if ee < 16 else inp["sw2"][ee - 16]
                    for sl in range(4):
                        conv.append(("g", g_src[sl * 512:(sl + 1) * 512, :], ee, sl))
                        conv.append(("w1", w1_src[sl * 512:(sl + 1) * 512, :], ee, sl))
                    for sl in range(4):
                        conv.append(("w2", w2_src[sl * 256:(sl + 1) * 256, :], ee, sl))
            if "p1" in phases:
                for sl in range(8):
                    conv.append(("fin", inp["final_w"][sl * 256:(sl + 1) * 256, :], 0, sl))
                for sl in range(8):
                    conv.append(("qc", inp["w_qc"][sl * 256:(sl + 1) * 256, :], 0, sl))
                for sl in range(2):
                    conv.append(("qup", inp["w_qup"][sl * 256:(sl + 1) * 256, :], 0, sl))
                for wi, wn in enumerate(("k_proj_w", "v_proj_w", "kvc_w", "kvc_wz")):
                    for sl in range(8):
                        conv.append(("kpv", inp[wn][sl * 256:(sl + 1) * 256, :], wi * 2048, sl))
                for wi, wn in enumerate(("idx_c_w", "idx_c_wz", "w_dq")):
                    for sl in range(8):
                        conv.append(("idx", inp[wn][sl * 256:(sl + 1) * 256, :], wi * 1024, sl))
                for sl in range(8):
                    conv.append(("ww", inp["w_w"][sl * 256:(sl + 1) * 256, :], 0, sl))
                for g_ in range(4):
                    for sl in range(2):
                        conv.append(("grp", inp["group_w"][g_][sl * 256:(sl + 1) * 256, :], g_, sl))
            if "p3" in phases:
                for sl in range(8):
                    conv.append(("wq", inp["Wq"][sl * 256:(sl + 1) * 256, :], 0, sl))
                    conv.append(("wo", inp["Wo"][sl * 256:(sl + 1) * 256, :], 0, sl))

            b_vC = get("vC")
            for ji, (kind, srcw, ee, sl) in enumerate(conv):
                rows, cols = srcw.shape
                ncol = rows * cols // 128
                lo = (ji % 2) * 4096
                subi, chi, dni = q_issue("dS")

                def syc(s, ji=ji, srcw=srcw, lo=lo, ncol=ncol, cols=cols, subi=subi, chi=chi):
                    s.wait_ge(sem["vC"], b_vC + max(ji - 1, 0))
                    s.wait_ge(sem[subi], chi)
                    s.dma_start(out=PF1[:, lo:lo + ncol].rearrange("p (c f) -> p c f", f=cols),
                                in_=srcw.rearrange("(c p) f -> p c f", p=128)).then_inc(sem[subi], 16)
                ew("sync", syc)

                st_prev = marks.get(("convst", ji - 1), [])
                def vec(v, ji=ji, lo=lo, ncol=ncol, subi=subi, dni=dni, st_prev=st_prev):
                    v.wait_ge(sem[subi], dni)
                    for ssub, sval in st_prev:
                        v.wait_ge(sem[ssub], sval)
                    v.tensor_copy(PB1[:, lo:lo + ncol], PF1[:, lo:lo + ncol]).then_inc(sem["vC"], 1)
                ew("vector", vec)
                add("vC", 1)

                outs = []
                stv = PB1[:, lo:lo + ncol]
                if kind == "g":
                    for hh in range(2):
                        dst_ap = WGs[ee, hh, :, sl * 4:(sl + 1) * 4, :]
                        srcv = stv.rearrange("p (c f) -> p c f", f=1024)[:, :, hh * 512:(hh + 1) * 512]
                        outs.append((dst_ap, srcv))
                elif kind == "w1":
                    for hh in range(2):
                        dst_ap = WW1s[ee, hh, :, sl * 4:(sl + 1) * 4, :]
                        srcv = stv.rearrange("p (c f) -> p c f", f=1024)[:, :, hh * 512:(hh + 1) * 512]
                        outs.append((dst_ap, srcv))
                elif kind == "w2":
                    for dh in range(2):
                        dst_ap = WW2s[ee, dh, :, sl * 2:(sl + 1) * 2, :]
                        srcv = stv.rearrange("p (c f) -> p c f", f=2048)[:, :, dh * 1024:(dh + 1) * 1024]
                        outs.append((dst_ap, srcv))
                elif kind == "qc":
                    outs.append((WPA[:, 0, sl * 1024:(sl + 1) * 1024], stv))
                elif kind == "qup":
                    outs.append((WPA[:, 1, sl * 4096:(sl + 1) * 4096], stv))
                elif kind == "kpv":
                    outs.append((WPA[:, 2, ee + sl * 256:ee + (sl + 1) * 256], stv))
                elif kind == "idx":
                    outs.append((WPA[:, 3, ee + sl * 128:ee + (sl + 1) * 128], stv))
                elif kind == "ww":
                    outs.append((WPA[:, 3, 3072 + sl * 8:3072 + (sl + 1) * 8], stv))
                elif kind == "grp":
                    outs.append((WGRPs[:, ee, sl * 2:(sl + 1) * 2, :], stv))
                elif kind == "fin":
                    dst_ap = WFINs[sl * 2:(sl + 1) * 2].rearrange("c p f -> p c f")
                    srcv = stv.rearrange("p (c f) -> p c f", f=2048)
                    outs.append((dst_ap, srcv))
                elif kind == "wo":
                    for dq in range(4):
                        dst_ap = WOs[dq, sl * 2:(sl + 1) * 2, :, :].rearrange("c p f -> p c f")
                        srcv = stv.rearrange("p (c f) -> p c f", f=2048)[:, :, dq * 512:(dq + 1) * 512]
                        outs.append((dst_ap, srcv))
                else:
                    for och in range(16):
                        dst_ap = WQs[och, :, sl * 2:(sl + 1) * 2, :]
                        srcv = stv.rearrange("p (c o h) -> p c o h", c=2, h=128)[:, :, och, :]
                        outs.append((dst_ap, srcv))
                stmarks = []
                for dst_ap, srcv in outs:
                    subg, chg, dng = q_issue("dG")
                    stmarks.append((subg, dng))
                    def gpc(g, ji=ji, dst_ap=dst_ap, srcv=srcv, subg=subg, chg=chg):
                        g.wait_ge(sem["vC"], b_vC + ji + 1)
                        g.wait_ge(sem[subg], chg)
                        g.dma_start(out=dst_ap, in_=srcv).then_inc(sem[subg], 16)
                    ew("gpsimd", gpc)
                marks[("convst", ji)] = stmarks
            marks["conv_vC"] = get("vC")
            marks["conv_dG_subs"] = [("dG0", 16 * ((Q["dG"] + 1) // 2)), ("dG1", 16 * (Q["dG"] // 2))]

            if need_ffn:
                c_cv = marks["conv_vC"]
                bl = []
                for _ in range(7):
                    bl.append(q_issue("dS"))
                def syb(s, c_cv=c_cv, bl=bl):
                    s.wait_ge(sem["vC"], c_cv)
                    dmas = [
                        (SM[0:16, 0:FF], inp["rgb"]),
                        (SM[16:18, 0:FF], inp["sgb"]),
                        (SM[32:48, 0:FF], inp["rb1"]),
                        (SM[48:50, 0:FF], inp["sb1"]),
                        (B2CAT[0:16, :], inp["rb2"]),
                        (B2CAT[16:18, :], inp["sb2"]),
                        (PF2[:, 0:256].rearrange("p (c e) -> p c e", e=16),
                         inp["wa"].rearrange("(c p) e -> p c e", p=128)),
                    ]
                    for (o, i_), (subq, chq, dnq) in zip(dmas, bl):
                        s.wait_ge(sem[subq], chq)
                        s.dma_start(out=o, in_=i_).then_inc(sem[subq], 16)
                ew("sync", syb)
                bias_done = [(subq, dnq) for (subq, chq, dnq) in bl]

                b_tM = get("tM"); b_vC2 = get("vC")
                def teb(t, b_tM=b_tM, b_vC2=b_vC2, bias_done=bias_done):
                    for subq, dnq in bias_done:
                        t.wait_ge(sem[subq], dnq)
                    t.wait_ge(sem["vC"], marks["const"])
                    for fc in range(8):
                        if fc >= 1:
                            t.wait_ge(sem["vC"], b_vC2 + fc)
                        t.transpose(psl[:, 0:64], SM[0:64, fc * 128:(fc + 1) * 128],
                                    identf[0:64, 0:64]).then_inc(sem["tM"], 1)
                ew("tensor", teb)
                for fc in range(8):
                    add("tM", 1)

                def veb(v, b_tM=b_tM):
                    for fc in range(8):
                        v.wait_ge(sem["tM"], b_tM + fc + 1)
                        v.tensor_copy(BIAS[:, fc * 32:fc * 32 + 18], psl[:, 0:18])
                        v.tensor_copy(BIAS[:, 256 + fc * 32:256 + fc * 32 + 18],
                                      psl[:, 32:50]).then_inc(sem["vC"], 1)
                ew("vector", veb)
                for fc in range(8):
                    add("vC", 1)

                def vew(v, bias_done=bias_done):
                    for subq, dnq in bias_done[6:]:
                        v.wait_ge(sem[subq], dnq)
                    v.tensor_copy(PB2[:, 16128:16384], PF2[:, 0:256]).then_inc(sem["vC"], 1)
                ew("vector", vew)
                add("vC", 1)
            if "p3" in phases:
                c_cv2 = get("vC")
                b3 = [q_issue("dS") for _ in range(6)]
                def syp3(s, c_cv2=c_cv2, b3=b3):
                    s.wait_ge(sem["vC"], c_cv2)
                    dmas = [
                        (PF1[:, 0:2048].rearrange("p (c h) -> p c h", h=128),
                         inp["Wk"].rearrange("(c p) h -> p c h", p=128)),
                        (PF1[:, 2048:4096].rearrange("p (c h) -> p c h", h=128),
                         inp["Wv"].rearrange("(c p) h -> p c h", p=128)),
                        (SM[0:16, 256:384], inp["bq"].rearrange("(c p) -> c p", p=128)),
                        (SM[16:17, 256:384], inp["bk"].rearrange("(o h) -> o h", o=1)),
                        (SM[17:18, 256:384], inp["bv"].rearrange("(o h) -> o h", o=1)),
                        (BOROW[0:1, :], inp["bo"].rearrange("(o d) -> o d", o=1)),
                    ]
                    for (o, i_), (subq, chq, dnq) in zip(dmas, b3):
                        s.wait_ge(sem[subq], chq)
                        s.dma_start(out=o, in_=i_).then_inc(sem[subq], 16)
                ew("sync", syp3)

                b_tM3 = get("tM")
                def tep3(t, b3=b3, b_tM3=b_tM3):
                    for subq, chq, dnq in b3:
                        t.wait_ge(sem[subq], dnq)
                    t.transpose(psl[:, 0:32], SM[0:32, 256:384], identf[0:32, 0:32]).then_inc(sem["tM"], 1)
                ew("tensor", tep3)
                add("tM", 1)

                def vep3(v, b3=b3, b_tM3=b_tM3):
                    for subq, chq, dnq in b3:
                        v.wait_ge(sem[subq], dnq)
                    v.tensor_copy(PB3[:, 6144:8192], PF1[:, 0:2048])
                    v.tensor_copy(PB3[:, 8192:10240], PF1[:, 2048:4096])
                    v.wait_ge(sem["tM"], b_tM3 + 1)
                    v.tensor_copy(SM[:, 232:250], psl[:, 0:18]).then_inc(sem["vC"], 1)
                ew("vector", vep3)
                add("vC", 1)
            marks["startup_vC"] = get("vC")

        NEA = ne_act
        VC_S, SC_S, TM_S, VS_S, SS_S = 96, 12 + 8 * NEA, 88 + 24 * NEA, 76, 4
        SY_S, SEV_S, SBC_S, DGQ_S = 8 * NEA, 16 * NEA, NEA, 8 + 6 * NEA

        def emit_ffn(src, dst):
            first_ffn = not marks.get("ffn_emitted", False)
            marks["ffn_emitted"] = True

            xnT = PB2[:, 0:8192]
            waB = PB2[:, 16128:16384]
            wbuf = [PB1[:, 0:8192], PB1[:, 8192:16384]]
            ybuf = [PB1[:, 16384:20480], PB1[:, 20480:24576]]
            out_acc = PF1[:, 0:8192]
            xtile = PF2[:, 0:4096]
            sqb = PF2[:, 4096:6144]
            geluh = [PB3[:, 0:2048], PB3[:, 2048:4096]]
            xn_tmp = PB3[:, 4096:6144]
            comb = SM[:, 0:128]
            lnt = SM[:, 128:160]
            affb = SM[:, 160:192]
            gb_cur = [SM[:, 192:200], SM[:, 200:208]]
            b1_cur = [SM[:, 208:216], SM[:, 216:224]]
            comb_cur = [SM[:, 224:228], SM[:, 228:232]]
            biasT3 = BIAS[:, 0:512].rearrange("p (q e) -> p q e", e=32)
            comb3 = SM[:, 0:128].rearrange("p (t e) -> p t e", e=32)

            bVC = get("vC"); bSC = get("sC"); bTM = get("tM"); bVS = get("vS")
            bSS = get("sS"); bSY = get("sY"); bSEV = get("sEV"); bSBC = get("sBC")
            baseG = Q["dG"]
            assert baseG % 2 == 0
            bG2 = baseG // 2
            src3 = src.rearrange("(n p) d -> n p d", p=128)
            dst3 = dst.rearrange("(n p) d -> n p d", p=128)

            xs4 = src3.rearrange("(s u) p d -> s u p d", u=4)
            ds4 = dst3.rearrange("(s u) p d -> s u p d", u=4)

            def gp_ffn(g):
                rt = wregr(g, "gp_rt")
                rsQ = wregr(g, "gp_rsQ")
                rsT = wregr(g, "gp_rsT")
                rsV = wregr(g, "gp_rsV")
                rsC = wregr(g, "gp_rsC")
                rsE = wregr(g, "gp_rsE")
                if first_ffn:
                    for subg, dval in marks["conv_dG_subs"]:
                        g.wait_ge(sem[subg], dval)
                with nc.Fori(0, NS, engines=[ET.Pool]) as s_:
                    s = urv(s_, NS - 1)
                    for r, k in [(rsQ, 928), (rsT, 520), (rsV, 96), (rsC, 156), (rsE, 288)]:
                        g.reg_mov(r, s)
                        g.reg_mul(r, r, k)
                    for t in range(4):
                        subx = "dG" + str((baseG + t) % 2)
                        if t < 2:
                            off = 20 * (t + 2) + 2 - VC_S
                            offs = 3 * (t + 2) + 1 - SC_S
                        else:
                            off = 20 * (t - 2) + 2
                            offs = 3 * (t - 2) + 1
                        wrg(g, rt, "vC", rsV, None, bVC + off, clamp=bVC)
                        wrg(g, rt, "sC", rsC, None, bSC + offs, clamp=bSC)
                        wrg(g, rt, subx, rsQ, None, 16 * (bG2 + t // 2), clamp=None)
                        g.dma_start(out=xtile[:, (t % 2) * 2048:(t % 2) * 2048 + 2048],
                                    in_=(src3[t] if "xstat" in KDROP else xs4[s, t])).then_inc(sem[subx], 16)
                    def gp_inner(g):
                        with nc.Fori(0, NEA, step=2, engines=[ET.Pool]) as e2_:
                            e2 = urv(e2_, NEA - 2)
                            for par in range(2):
                                j = e2 + par
                                for k in range(6):
                                    subw = "dG" + str((baseG + k) % 2)
                                    wrg(g, rt, "tM", rsT, (j, 24), bTM + 88 + 4 * k - 4, clamp=bTM)
                                    wrg(g, rt, subw, rsQ, (j, 48), 16 * (bG2 + 2 + k // 2), clamp=None)
                                    if k in (0, 2):
                                        src_ap = WGs[j, k // 2]
                                        dst_ap = wbuf[k % 2][:, 0:8192].rearrange("p (c f) -> p c f", c=16)
                                    elif k in (1, 3):
                                        src_ap = WW1s[j, (k - 1) // 2]
                                        dst_ap = wbuf[k % 2][:, 0:8192].rearrange("p (c f) -> p c f", c=16)
                                    else:
                                        src_ap = WW2s[j, k - 4]
                                        dst_ap = wbuf[k % 2][:, 0:8192].rearrange("p (c f) -> p c f", c=8)
                                    g.dma_start(out=dst_ap, in_=src_ap).then_inc(sem[subw], 16)
                        free_snap(e2_)
                    if NEA > 0:
                        gp_inner(g)
                    for t in range(4):
                        subs_ = "dG" + str((baseG + t) % 2)
                        wrg(g, rt, "sEV", rsE, None, bSEV + SEV_S, clamp=None)
                        wrg(g, rt, subs_, rsQ, None, 16 * (bG2 + (112 + t) // 2), clamp=None)
                        g.dma_start(out=(dst3[t] if "ststat" in KDROP else ds4[s, t]),
                                    in_=out_acc[:, t * 2048:(t + 1) * 2048]).then_inc(sem[subs_], 16)
                free_snap(s_)
            KSOLO = os.environ.get("KSOLO", "")

            def drv(eng, specs):
                def fn(e):
                    for s_i in range(NS):
                        for nm, cntr, inc in specs:
                            for _ in range(cntr):
                                e.nop().then_inc(sem[nm], inc)
                                e.nop(); e.nop(); e.nop()
                return fn

            def gp_drv(e):
                for s_i in range(NS):
                    for pos in range(DGQ_S):
                        sub = "dG" + str((baseG + pos) % 2)
                        for _ in range(16):
                            e.nop().then_inc(sem[sub], 1)
                            e.nop(); e.nop(); e.nop()

            if KSOLO and "gp" not in KSOLO:
                ew("gpsimd", gp_drv)
            else:
                ew("gpsimd", gp_ffn)
            Q["dG"] += NS * DGQ_S

            def sc_ffn(sc):
                rt = wregr(sc, "sc_rt")
                rsT = wregr(sc, "sc_rsT"); rsV = wregr(sc, "sc_rsV")
                rsC = wregr(sc, "sc_rsC"); rsY = wregr(sc, "sc_rsY")
                rsQ = wregr(sc, "sc_rsQ"); rsB = wregr(sc, "sc_rsB"); rsS4 = wregr(sc, "sc_rsS4")
                with nc.Fori(0, NS, engines=[ET.Activation]) as s_:
                    s = urv(s_, NS - 1)
                    for r, k in [(rsT, 520), (rsV, 96), (rsC, 156), (rsY, 144), (rsQ, 928), (rsB, 18), (rsS4, 4)]:
                        sc.reg_mov(r, s)
                        sc.reg_mul(r, r, k)
                    nS0 = ECNT["sS"]
                    nC0 = ECNT["sCc"]
                    for t in range(4):
                        subx = "dG" + str((baseG + t) % 2)
                        off = (20 * (t - 1) + 1) if t >= 1 else (20 * 3 + 1 - VC_S)
                        wrg(sc, rt, "vC", rsV, None, bVC + off, clamp=bVC)
                        wrg(sc, rt, subx, rsQ, None, 16 * (bG2 + t // 2 + 1), clamp=None)
                        (sc.copy(sqb[:, 0:1], sbc[:, 2:3]) if "sact" in KDROP else
                         sc.activation(sqb[:, 0:2048], xtile[:, (t % 2) * 2048:(t % 2) * 2048 + 2048],
                                       AF.Square)).then_inc(sem["sC"], 1)
                        ECNT["sCc"] += 1
                        wrg(sc, rt, "vC", rsV, None, bVC + 20 * t + 1, clamp=None)
                        (sc.copy(lnt[:, 6:7], sbc[:, 2:3]) if "sact" in KDROP else
                         sc.activation(lnt[:, 6:7], lnt[:, 5:6], AF.Ln,
                                       bias=sbc[:, 0:1], scale=1.0)).then_inc(sem["sS"], 1)
                        ECNT["sS"] += 1
                        wrg(sc, rt, "sS", rsS4, None, bSS + t + 1, clamp=None)
                        (sc.copy(lnt[:, 7:8], sbc[:, 2:3]) if "sact" in KDROP else
                         sc.activation(lnt[:, 7:8], lnt[:, 6:7], AF.Exp,
                                       bias=0.0, scale=-0.5)).then_inc(sem["sC"], 1)
                        ECNT["sCc"] += 1
                        offa = (20 * (t - 1) + 19) if t >= 1 else (20 * 3 + 19 - VC_S)
                        wrg(sc, rt, "vC", rsV, None, bVC + offa, clamp=bVC)
                        wrg(sc, rt, "tM", rsT, None, bTM + 18 * t + 17, clamp=None)
                        (sc.copy(affb[:, 0:16], sbc[:, 2:3].to_broadcast([128, 16])) if "sact" in KDROP else
                         sc.activation(affb[:, 0:16], ps[4][:, 0:16], AF.Exp,
                                       bias=0.0, scale=-1.0)).then_inc(sem["sC"], 1)
                        ECNT["sCc"] += 1
                    def sc_inner(sc):
                        with nc.Fori(0, NEA, step=2, engines=[ET.Activation]) as e2_:
                            e2 = urv(e2_, NEA - 2)
                            for par in range(2):
                                j = e2 + par
                                wrg(sc, rt, "sBC", rsB, (j, 1), bSBC + 1, clamp=None)
                                for hh in range(2):
                                    for fcl in range(4):
                                        wrg(sc, rt, "tM", rsT, (j, 24), bTM + 88 + hh * 8 + fcl + 1, clamp=None)
                                        wrg(sc, rt, "sY", rsY, (j, 8), bSY + 4 * hh + fcl - 7, clamp=bSY)
                                        sc.activation(geluh[hh][:, fcl * 512:(fcl + 1) * 512],
                                                      ps[fcl % 2][:, :], AF.Gelu,
                                                      bias=gb_cur[par][:, (hh * 4 + fcl):(hh * 4 + fcl) + 1],
                                                      scale=1.0).then_inc(sem["sC"], 1)
                                        ECNT["sCc"] += 1
                        free_snap(e2_)
                    if NEA > 0:
                        sc_inner(sc)
                free_snap(s_)
            if KSOLO and "sc" not in KSOLO:
                ew("scalar", drv(None, [("sC", SC_S, 1), ("sS", SS_S, 1)]))
            else:
                ew("scalar", sc_ffn)

            def ve_ffn(v):
                rt = wregr(v, "ve_rt")
                rsT = wregr(v, "ve_rsT"); rsV = wregr(v, "ve_rsV"); rsC = wregr(v, "ve_rsC")
                rsS = wregr(v, "ve_rsS"); rsQ = wregr(v, "ve_rsQ"); rsE = wregr(v, "ve_rsE")
                rsY = wregr(v, "ve_rsY")
                if first_ffn:
                    for subg, dval in marks["conv_dG_subs"]:
                        v.wait_ge(sem[subg], dval)
                with nc.Fori(0, NS, engines=[ET.DVE]) as s_:
                    s = urv(s_, NS - 1)
                    for r, k in [(rsT, 520), (rsV, 96), (rsC, 156), (rsS, 76), (rsQ, 928), (rsE, 288), (rsY, 144)]:
                        v.reg_mov(r, s)
                        v.reg_mul(r, r, k)
                    nV0 = ECNT["vCc"]; nS0 = ECNT["vS"]
                    for t in range(4):
                        subx = "dG" + str((baseG + t) % 2)
                        wrg(v, rt, subx, rsQ, None, 16 * (bG2 + t // 2 + 1), clamp=None)
                        wrg(v, rt, "vS", rsS, None, bVS + 19 * t - 19 + 8, clamp=bVS)
                        (v.memset(lnt[:, 0:1], 1.0) if "ln" in KDROP else
                         v.tensor_reduce(lnt[:, 0:1], xtile[:, (t % 2) * 2048:(t % 2) * 2048 + 2048],
                                         axis=AX.X, op=ALU.add)).then_inc(sem["vS"], 1)
                        ECNT["vS"] += 1
                        wrg(v, rt, "vS", rsS, None, bVS + 19 * t + 1, clamp=None)
                        v.tensor_scalar(lnt[:, 1:2], lnt[:, 0:1], -1.0 / D, None, op0=ALU.mult).then_inc(sem["vS"], 1)
                        ECNT["vS"] += 1
                        wrg(v, rt, "sC", rsC, None, bSC + 3 * t + 1, clamp=None)
                        (v.memset(lnt[:, 2:3], 1.0) if "ln" in KDROP else
                         v.tensor_reduce(lnt[:, 2:3], sqb[:, 0:2048], axis=AX.X, op=ALU.add)).then_inc(sem["vS"], 1)
                        ECNT["vS"] += 1
                        wrg(v, rt, "vS", rsS, None, bVS + 19 * t + 3, clamp=None)
                        v.tensor_scalar(lnt[:, 3:4], lnt[:, 2:3], 1.0 / D, None, op0=ALU.mult).then_inc(sem["vS"], 1)
                        ECNT["vS"] += 1
                        v.tensor_tensor(lnt[:, 4:5], lnt[:, 1:2], lnt[:, 1:2], op=ALU.mult).then_inc(sem["vS"], 1)
                        ECNT["vS"] += 1
                        wrg(v, rt, "vS", rsS, None, bVS + 19 * t + 5, clamp=None)
                        v.tensor_tensor(lnt[:, 5:6], lnt[:, 3:4], lnt[:, 4:5],
                                        op=ALU.subtract).then_inc(sem["vC"], 1)
                        ECNT["vCc"] += 1
                        wrg(v, rt, "sC", rsC, None, bSC + 3 * t + 2, clamp=None)
                        offx = (18 * (t - 1) + 16) if t >= 1 else (18 * 3 + 16 - TM_S)
                        wrg(v, rt, "tM", rsT, None, bTM + offx, clamp=bTM)
                        v.tensor_scalar(xn_tmp[:], xtile[:, (t % 2) * 2048:(t % 2) * 2048 + 2048],
                                        lnt[:, 1:2], lnt[:, 7:8], op0=ALU.add, op1=ALU.mult).then_inc(sem["vS"], 1)
                        ECNT["vS"] += 1
                        v.tensor_scalar(lnt[:, 8:9], lnt[:, 7:8], 1.0, None, op0=ALU.add).then_inc(sem["vS"], 1)
                        ECNT["vS"] += 1
                        v.tensor_tensor(lnt[:, 9:10], lnt[:, 1:2], lnt[:, 7:8], op=ALU.mult).then_inc(sem["vS"], 1)
                        ECNT["vS"] += 1
                        wrg(v, rt, "dG0", rsQ, None, 16 * bG2, clamp=None)
                        wrg(v, rt, "dG1", rsQ, None, 16 * bG2, clamp=None)
                        wrg(v, rt, "vS", rsS, None, bVS + 19 * t + 8, clamp=None)
                        v.tensor_scalar(out_acc[:, t * 2048:(t + 1) * 2048],
                                        xtile[:, (t % 2) * 2048:(t % 2) * 2048 + 2048],
                                        lnt[:, 8:9], lnt[:, 9:10], op0=ALU.mult, op1=ALU.add).then_inc(sem["vC"], 1)
                        ECNT["vCc"] += 1
                        wrg(v, rt, "tM", rsT, None, bTM, clamp=None)
                        for c in range(16):
                            slot = c % 4
                            wrg(v, rt, "tM", rsT, None, bTM + 18 * t + c + 1, clamp=None)
                            v.tensor_copy(xnT[:, c * 512 + t * 128: c * 512 + t * 128 + 128],
                                          pst[:, slot * 128:(slot + 1) * 128]).then_inc(sem["vC"], 1)
                            ECNT["vCc"] += 1
                        wrg(v, rt, "sC", rsC, None, bSC + 3 * t + 3, clamp=None)
                        a = affb[:, 0:16]
                        seqv = [
                            (lambda: v.tensor_scalar(a, a, 1.0, None, op0=ALU.add)),
                            (lambda: v.reciprocal(a, a)),
                            (lambda: v.tensor_reduce(lnt[:, 10:11], a, axis=AX.X, op=ALU.max)),
                            (lambda: v.tensor_scalar(affb[:, 16:32], a, lnt[:, 10:11], None, op0=ALU.is_equal)),
                            (lambda: v.scalar_tensor_tensor(affb[:, 16:32], in0=affb[:, 16:32], scalar=-1.0,
                                                            in1=a, op0=ALU.mult, op1=ALU.add)),
                            (lambda: v.tensor_reduce(lnt[:, 11:12], affb[:, 16:32], axis=AX.X, op=ALU.max)),
                            (lambda: v.tensor_tensor(lnt[:, 12:13], lnt[:, 10:11], lnt[:, 11:12], op=ALU.add)),
                            (lambda: v.reciprocal(lnt[:, 13:14], lnt[:, 12:13])),
                            (lambda: v.tensor_scalar(affb[:, 16:32], a, lnt[:, 11:12], None, op0=ALU.is_ge)),
                            (lambda: v.tensor_tensor(affb[:, 16:32], affb[:, 16:32], a, op=ALU.mult)),
                            (lambda: v.tensor_scalar(comb[:, t * 32:t * 32 + 16], affb[:, 16:32],
                                                     lnt[:, 13:14], None, op0=ALU.mult)),
                        ]
                        for qi_, fn in enumerate(seqv):
                            if qi_ > 0:
                                wrg(v, rt, "vS", rsS, None, bVS + 19 * t + 8 + qi_, clamp=None)
                            if "vg" in KDROP:
                                v.memset(lnt[:, 14:15], 0.5).then_inc(sem["vS"], 1)
                            else:
                                fn().then_inc(sem["vS"], 1)
                            ECNT["vS"] += 1
                        wrg(v, rt, "vS", rsS, None, bVS + 19 * t + 19, clamp=None)
                        v.memset(comb[:, t * 32 + 16:t * 32 + 18], 1.0).then_inc(sem["vC"], 1)
                        ECNT["vCc"] += 1
                        wrg(v, rt, "tM", rsT, None, bTM + 18 * t + 18, clamp=None)
                        v.tensor_copy(COMBT[0:18, t * 128:(t + 1) * 128],
                                      psl[0:18, 0:128]).then_inc(sem["vC"], 1)
                        ECNT["vCc"] += 1
                    for t in range(4):
                        for nchk in range(4):
                            kk = t * 4 + nchk
                            wrg(v, rt, "tM", rsT, None, bTM + 72 + kk + 1, clamp=None)
                            dsl = out_acc[:, t * 2048 + nchk * 512: t * 2048 + nchk * 512 + 512]
                            v.tensor_tensor(dsl, dsl, ps[5][:, :], op=ALU.add).then_inc(sem["vC"], 1)
                            ECNT["vCc"] += 1
                    def ve_inner(v):
                        with nc.Fori(0, NEA, step=2, engines=[ET.DVE]) as e2_:
                            e2 = urv(e2_, NEA - 2)
                            for par in range(2):
                                j = e2 + par
                                ybf = ybuf[par]
                                wrg(v, rt, "sC", rsC, (j, 8), bSC + 12 - 8, clamp=bSC)
                                v.tensor_copy(gb_cur[par], biasT3[:, 0:8, :][:, :, j])
                                v.tensor_copy(b1_cur[par], biasT3[:, 8:16, :][:, :, j])
                                v.tensor_copy(comb_cur[par], comb3[:, :, j]).then_inc(sem["sBC"], 1)
                                for hh in range(2):
                                    for fcl in range(4):
                                        fcg = hh * 4 + fcl
                                        wrg(v, rt, "tM", rsT, (j, 24), bTM + 88 + hh * 8 + 4 + fcl + 1, clamp=None)
                                        wrg(v, rt, "sC", rsC, (j, 8), bSC + 12 + hh * 4 + fcl + 1, clamp=None)
                                        wrg(v, rt, "tM", rsT, (j, 24), bTM + 88 - 24, clamp=bTM)
                                        v.scalar_tensor_tensor(
                                            ybf[:, fcg * 512: fcg * 512 + 512],
                                            in0=ps[2 + fcl % 2][:, :],
                                            scalar=b1_cur[par][:, fcg:fcg + 1],
                                            in1=geluh[hh][:, fcl * 512:(fcl + 1) * 512],
                                            op0=ALU.add, op1=ALU.mult).then_inc(sem["sY"], 1)
                                wrg(v, rt, "sEV", rsE, (j, 16), bSEV, clamp=None)
                                for dh in range(2):
                                    for t in range(4):
                                        wrg(v, rt, "tM", rsT, (j, 24), bTM + 88 + 16 + dh * 4 + t + 1, clamp=None)
                                        bp = 4 if (dh * 4 + t) % 2 == 0 else 2
                                        for nch in range(2):
                                            dsl = out_acc[:, t * 2048 + dh * 1024 + nch * 512:
                                                          t * 2048 + dh * 1024 + nch * 512 + 512]
                                            v.scalar_tensor_tensor(
                                                dsl, in0=ps[bp + nch][:, :],
                                                scalar=comb_cur[par][:, t:t + 1],
                                                in1=dsl, op0=ALU.mult, op1=ALU.add).then_inc(sem["sEV"], 1)
                        free_snap(e2_)
                    if NEA > 0:
                        ve_inner(v)
                free_snap(s_)
            if KSOLO and "ve" not in KSOLO:
                ew("vector", drv(None, [("vC", VC_S, 1), ("vS", VS_S, 1), ("sBC", SBC_S, 1),
                                        ("sY", SY_S, 1), ("sEV", SEV_S, 1)]))
            else:
                ew("vector", ve_ffn)

            def te_ffn(t_):
                rt = wregr(t_, "te_rt")
                rsT = wregr(t_, "te_rsT"); rsV = wregr(t_, "te_rsV"); rsC = wregr(t_, "te_rsC")
                rsE = wregr(t_, "te_rsE"); rsY = wregr(t_, "te_rsY"); rsQ = wregr(t_, "te_rsQ")
                with nc.Fori(0, NS, engines=[ET.PE]) as s_:
                    s = urv(s_, NS - 1)
                    for r, k in [(rsT, 520), (rsV, 96), (rsC, 156), (rsE, 288), (rsY, 144), (rsQ, 928)]:
                        t_.reg_mov(r, s)
                        t_.reg_mul(r, r, k)
                    nT0 = ECNT["tMc"]
                    for t in range(4):
                        wrg(t_, rt, "vC", rsV, None, bVC + 20 * t + 2, clamp=None)
                        for c in range(16):
                            slot = c % 4
                            if c >= 1:
                                wrg(t_, rt, "vC", rsV, None, bVC + 20 * t + 2 + c, clamp=None)
                            if "tposeX" in KDROP:
                                t_.nop().then_inc(sem["tM"], 1)
                            else:
                                t_.transpose(pst[:, slot * 128:(slot + 1) * 128],
                                             xn_tmp[:, c * 128:(c + 1) * 128], identb[:]).then_inc(sem["tM"], 1)
                            ECNT["tMc"] += 1
                        wrg(t_, rt, "vC", rsV, None, bVC + 20 * t + 18, clamp=None)
                        wrg(t_, rt, "sEV", rsE, None, bSEV, clamp=None)
                        offsg = (3 * (t - 1) + 3) if t >= 1 else (3 * 3 + 3 - SC_S)
                        wrg(t_, rt, "sC", rsC, None, bSC + offsg, clamp=bSC)
                        if "gate" in KDROP:
                            t_.nop().then_inc(sem["tM"], 1)
                        else:
                            for c in range(16):
                                ins = t_.matmul(ps[4][:, 0:16],
                                                xnT[:, c * 512 + t * 128: c * 512 + t * 128 + 128],
                                                waB[:, c * 16:(c + 1) * 16],
                                                start=(c == 0), stop=(c == 15))
                            ins.then_inc(sem["tM"], 1)
                        ECNT["tMc"] += 1
                        wrg(t_, rt, "vC", rsV, None, bVC + 20 * t + 19, clamp=None)
                        if "tposeC" in KDROP:
                            t_.nop().then_inc(sem["tM"], 1)
                        else:
                            t_.transpose(psl[0:32, 0:128], comb[:, t * 32:(t + 1) * 32],
                                         identf[:]).then_inc(sem["tM"], 1)
                        ECNT["tMc"] += 1
                    for t in range(4):
                        for nchk in range(4):
                            kk = t * 4 + nchk
                            wrg(t_, rt, "vC", rsV, None, bVC + 80 + kk, clamp=None)
                            if "b2mm" in KDROP:
                                t_.nop().then_inc(sem["tM"], 1)
                            else:
                                t_.matmul(ps[5][:, :], COMBT[0:18, t * 128:(t + 1) * 128],
                                          B2CAT[0:18, nchk * 512:(nchk + 1) * 512],
                                          start=True, stop=True).then_inc(sem["tM"], 1)
                            ECNT["tMc"] += 1
                    wrg(t_, rt, "vC", rsV, None, bVC + VC_S, clamp=None)
                    wrg(t_, rt, "sC", rsC, None, bSC + 12, clamp=None)
                    def te_inner(t_):
                        with nc.Fori(0, NEA, step=2, engines=[ET.PE]) as e2_:
                            e2 = urv(e2_, NEA - 2)
                            for par in range(2):
                                j = e2 + par
                                ybf = ybuf[par]
                                for hh in range(2):
                                    ku = hh * 2
                                    for fcl in range(4):
                                        wrg(t_, rt, "dG" + str((baseG + ku) % 2), rsQ, (j, 48),
                                            16 * (bG2 + 2 + ku // 2 + 1), clamp=None)
                                        wrg(t_, rt, "sC", rsC, (j, 8), bSC + 12 + hh * 4 + fcl - 1, clamp=bSC)
                                        buf = wbuf[ku % 2]
                                        for dc in range(16):
                                            ins = t_.matmul(ps[fcl % 2][:, :],
                                                            buf[:, dc * 512 + fcl * 128: dc * 512 + fcl * 128 + 128],
                                                            xnT[:, dc * 512:(dc + 1) * 512],
                                                            start=(dc == 0), stop=(dc == 15))
                                        ins.then_inc(sem["tM"], 1)
                                        ECNT["tMc"] += 1
                                    ku = hh * 2 + 1
                                    for fcl in range(4):
                                        wrg(t_, rt, "dG" + str((baseG + ku) % 2), rsQ, (j, 48),
                                            16 * (bG2 + 2 + ku // 2 + 1), clamp=None)
                                        wrg(t_, rt, "sY", rsY, (j, 8), bSY + hh * 4 + fcl - 1, clamp=bSY)
                                        buf = wbuf[ku % 2]
                                        for dc in range(16):
                                            ins = t_.matmul(ps[2 + fcl % 2][:, :],
                                                            buf[:, dc * 512 + fcl * 128: dc * 512 + fcl * 128 + 128],
                                                            xnT[:, dc * 512:(dc + 1) * 512],
                                                            start=(dc == 0), stop=(dc == 15))
                                        ins.then_inc(sem["tM"], 1)
                                        ECNT["tMc"] += 1
                                wrg(t_, rt, "sY", rsY, (j, 8), bSY + 8, clamp=None)
                                for dh in range(2):
                                    ku = 4 + dh
                                    wrg(t_, rt, "dG" + str((baseG + ku) % 2), rsQ, (j, 48),
                                        16 * (bG2 + 2 + ku // 2 + 1), clamp=None)
                                    buf = wbuf[ku % 2]
                                    for t in range(4):
                                        wrg(t_, rt, "sEV", rsE, (j, 16), bSEV + 2 * (dh * 4 + t - 1), clamp=bSEV)
                                        bp = 4 if (dh * 4 + t) % 2 == 0 else 2
                                        for nch in range(2):
                                            for fc in range(8):
                                                ins = t_.matmul(ps[bp + nch][:, :],
                                                                ybf[:, fc * 512 + t * 128: fc * 512 + t * 128 + 128],
                                                                buf[:, fc * 1024 + nch * 512: fc * 1024 + nch * 512 + 512],
                                                                start=(fc == 0), stop=(fc == 7))
                                        ins.then_inc(sem["tM"], 1)
                                        ECNT["tMc"] += 1
                        free_snap(e2_)
                    if NEA > 0:
                        te_inner(t_)
                free_snap(s_)
            if KSOLO and "te" not in KSOLO:
                ew("tensor", drv(None, [("tM", TM_S, 1)]))
            else:
                ew("tensor", te_ffn)

            add("vC", NS * VC_S); add("sC", NS * SC_S); add("tM", NS * TM_S)
            add("vS", NS * VS_S); add("sS", NS * SS_S); add("sY", NS * SY_S)
            add("sEV", NS * SEV_S); add("sBC", NS * SBC_S)
            marks[("ffn_final", id(dst))] = [
                ("dG0", 16 * ((Q["dG"] + 1) // 2)), ("dG1", 16 * (Q["dG"] // 2))]
            marks["last_texM"] = get("tM")


        def emit_attna(src, dst):
            NB = T // 16
            NT = T // 128
            NCH = (NB + 127) // 128
            CNT = {"tM": get("tM"), "vC": get("vC"), "sC": get("sC")}
            OPS = {k: [] for k in ("sync", "vector", "scalar", "tensor", "gpsimd")}

            def op(ek, fn, inc=None, waits=()):
                def wrapped(e, fn=fn, inc=inc):
                    ins = fn(e)
                    if inc is not None:
                        ins.then_inc(sem[inc], 1)
                OPS[ek].append((tuple(waits), wrapped))
                if inc is not None:
                    CNT[inc] += 1

            def vop(fn, waits=()):
                op("vector", fn, "vC", (("vC", CNT["vC"]),) + tuple(waits))

            def scop(fn, waits=()):
                op("scalar", fn, "sC", (("sC", CNT["sC"]),) + tuple(waits))

            def top(fn, waits=(), inc=True):
                op("tensor", fn, "tM" if inc else None, (("tM", CNT["tM"]),) + tuple(waits))

            def dma(qname, ek, outap, inap, waits=()):
                sub, ch, dn = q_issue(qname)
                def fn(e, outap=outap, inap=inap, sub=sub):
                    e.dma_start(out=outap, in_=inap).then_inc(sem[sub], 16)
                OPS[ek].append((tuple(waits) + ((sub, ch),), fn))
                return (sub, dn)

            def TM():
                return ("tM", CNT["tM"])

            def VC():
                return ("vC", CNT["vC"])

            def SC():
                return ("sC", CNT["sC"])

            ENTRY = (("tM", CNT["tM"]), ("vC", CNT["vC"]), ("sC", CNT["sC"]))

            MBT = PB1[:, 0:2 * T]
            WS1 = PB1[:, 8192:16384]
            WS2 = PB1[:, 16384:24576]
            xnT = PB2[:, 0:8192]
            xn_tmp = PB3[:, 4096:6144]
            qcb = PB2[:, 8192:10240]
            qstS = [PB2[:, 10240:10752], PB2[:, 10752:11264]]
            vT_tmp = PB2[:, 11264:11776]
            qdb = PB2[:, 11776:12288]
            qib = PB2[:, 12288:14336]
            kcompT = PB2[:, 14336:14592]
            ckvnT = PB2[:, 14592:14848]
            ckv_tok = PB2[:, 14848:15104]
            mbb = PB2[:, 15632:15888]
            qhbS = [PB2[:, 0:512], PB2[:, 512:1024]]
            pbS = [PB2[:, 1024:1536], PB2[:, 1536:2048]]
            oT = PB2[:, 0:8192]
            xtile = PF2[:, 0:4096]
            sqb = PF2[:, 4096:6144]
            CMB = PF2[:, 4096:4352]
            CBB = PF2[:, 4352:4608]
            ONE1 = PF2[:, 4608:4864]
            SWMB = PF2[:, 0:1024]
            SWM0B = PF2[:, 1024:2048]
            BSPR = PF2[0:1, 2048:2560]
            BWR = PF2[0:1, 2560:3072]
            QRS = PF2[0:1, 3072:3584]
            XRES = [PF2[:, 2048:2560], PF2[:, 2560:3072]]
            YSL = [PF2[:, 3072:3584], PF2[:, 3584:4096]]
            CKVT = PF1[:, 0:256]
            ZB = PF1[:, 256:768]
            CB = PF1[:, 768:1280]
            TR1 = PF1[:, 1280:1536]
            TR2 = PF1[:, 1536:1664]
            TR3 = PF1[:, 1664:1728]
            ZMAX = PF1[:, 1728:1760]
            ZSUM = PF1[:, 1760:1792]
            RS = PF1[:, 1792:1824]
            ACCB = PF1[:, 1824:1856]
            BR1 = PF1[:, 2048:2560]
            BR2 = PF1[:, 2560:3072]
            ISC = PF1[:, 3072:3328]
            TMPM = PF1[:, 3328:3584]
            MSK = PF1[:, 3584:3840]
            WIH = PF1[:, 3840:3856]
            KRO = PF1[:, 4096:4608]
            SQ2 = PF1[:, 4608:5120]
            COSB = PF1[:, 5120:5632]
            SINB = PF1[:, 5632:6144]
            RLA = PF1[:, 6144:6656]
            RLB2 = PF1[:, 6656:7168]
            AT1 = PF1[:, 7168:7680]
            AT2 = PF1[:, 7680:8192]
            TC1 = PF1[0:32, 6144:6656]
            TC2 = PF1[0:32, 6656:7168]
            TRB = PF1[0:32, 7680:8192]
            lnt = SM[:, 128:160]
            LO = SM[:, 160:161]; HIc = SM[:, 161:162]; CNTC = SM[:, 162:163]
            FBL = SM[:, 163:164]; MIDc = SM[:, 164:165]; D1 = SM[:, 165:166]
            D2 = SM[:, 166:167]; V64 = SM[:, 167:168]; CNTG = SM[:, 168:169]
            KPR = SM[:, 169:170]
            QNW = SM[:, 176:177]; KNW = SM[:, 177:178]
            LBSP = SM[0:1, 224:225]; LBW = SM[0:1, 225:226]
            MAXC = SM[0:1, 226:227]; TMPC = SM[0:1, 227:228]
            kswT = PE1[:, 0:4352]
            vsw = PE1[:, 4352:8704]

            src3 = src.rearrange("(n p) d -> n p d", p=128)
            dst3 = dst.rearrange("(n p) d -> n p d", p=128)

            d_qn = dma("dS", "sync", QNW, inp["q_norm_w"].rearrange("(p o) -> p o", o=1), waits=ENTRY)
            d_kn = dma("dS", "sync", KNW, inp["k_norm_w"].rearrange("(p o) -> p o", o=1), waits=ENTRY)
            d_kb = dma("dS", "sync", ZB[0:16, 0:128], inp["kvc_bias"], waits=ENTRY)
            d_ib = dma("dS", "sync", ZB[0:16, 128:192], inp["idx_c_bias"], waits=ENTRY)
            d_iu = dma("dS", "sync", CB[0:64, 0:256], inp["w_iuq"], waits=ENTRY)
            vop(lambda v: v.memset(CKVT, 0.0), waits=ENTRY)
            vop(lambda v: v.tensor_copy(PB3[0:64, 0:256], CB[0:64, 0:256]), waits=(d_iu,))
            top(lambda t: t.transpose(psl[:, 0:128], ZB[:, 0:128], identf[:]), waits=(d_kb, VC()))
            w_tp = TM()
            vop(lambda v: v.tensor_copy(TR1[:, 0:16], psl[:, 0:16]), waits=(w_tp,))
            w_cp = VC()
            for r in range(32):
                vop(lambda v, r=r: v.tensor_copy(BR1[:, r * 16:(r + 1) * 16], TR1[:, 0:16]), waits=(w_cp,))
            top(lambda t: t.transpose(psl[:, 0:128], ZB[:, 128:256], identf[:]), waits=(d_ib, VC()))
            w_tp2 = TM()
            vop(lambda v: v.tensor_copy(TR1[:, 16:32], psl[0:128, 0:16]), waits=(w_tp2,))
            w_cp2 = VC()
            for r in range(32):
                vop(lambda v, r=r: v.tensor_copy(BR2[0:64, r * 16:(r + 1) * 16], TR1[0:64, 16:32]), waits=(w_cp2,))
            tab_done = VC()

            LRW1 = ENTRY[0]
            LRW2 = ENTRY[0]
            LRXT = [ENTRY[1], ENTRY[1]]
            LRQST = [None, None]
            BANKV = {i: ("vC", 0) for i in range(6)}

            for s in range(NS):
                d_w0 = dma("dG", "gpsimd", WS1, WPA[:, 0, :], waits=(LRW1,))
                d_cos = dma("dS", "sync", COSB[0:32, :], inp["cosT"][:, s * 512:(s + 1) * 512], waits=(VC(),))
                d_sin = dma("dS", "sync", SINB[0:32, :], inp["sinT"][:, s * 512:(s + 1) * 512], waits=(VC(),))
                dx = {}
                for t4 in range(4):
                    dx[t4] = dma("dS", "sync", xtile[:, (t4 % 2) * 2048:(t4 % 2) * 2048 + 2048],
                                 src3[s * 4 + t4], waits=(LRXT[t4 % 2],))
                    scop(lambda sc, t4=t4: sc.activation(sqb[:, 0:2048],
                         xtile[:, (t4 % 2) * 2048:(t4 % 2) * 2048 + 2048], AF.Square), waits=(dx[t4], VC()))
                    w_sq = SC()
                    vop(lambda v, t4=t4: v.tensor_reduce(lnt[:, 0:1],
                        xtile[:, (t4 % 2) * 2048:(t4 % 2) * 2048 + 2048], axis=AX.X, op=ALU.add), waits=(dx[t4],))
                    vop(lambda v: v.tensor_scalar(lnt[:, 1:2], lnt[:, 0:1], -1.0 / D, None, op0=ALU.mult), waits=(VC(),))
                    vop(lambda v: v.tensor_reduce(lnt[:, 2:3], sqb[:, 0:2048], axis=AX.X, op=ALU.add), waits=(w_sq,))
                    vop(lambda v: v.tensor_scalar(lnt[:, 3:4], lnt[:, 2:3], 1.0 / D, None, op0=ALU.mult), waits=(VC(),))
                    vop(lambda v: v.tensor_tensor(lnt[:, 4:5], lnt[:, 1:2], lnt[:, 1:2], op=ALU.mult))
                    vop(lambda v: v.tensor_tensor(lnt[:, 5:6], lnt[:, 3:4], lnt[:, 4:5], op=ALU.subtract), waits=(VC(),))
                    w_var = VC()
                    scop(lambda sc: sc.activation(lnt[:, 6:7], lnt[:, 5:6], AF.Ln, bias=sbc[:, 0:1], scale=1.0), waits=(w_var,))
                    scop(lambda sc: sc.activation(lnt[:, 7:8], lnt[:, 6:7], AF.Exp, bias=0.0, scale=-0.5), waits=(SC(),))
                    w_rstd = SC()
                    vop(lambda v, t4=t4: v.tensor_scalar(xn_tmp[:],
                        xtile[:, (t4 % 2) * 2048:(t4 % 2) * 2048 + 2048], lnt[:, 1:2], lnt[:, 7:8],
                        op0=ALU.add, op1=ALU.mult), waits=(w_rstd, TM()))
                    LRXT[t4 % 2] = VC()
                    w_xn = VC()
                    for c in range(16):
                        slot = c % 4
                        top(lambda t, slot=slot, c=c: t.transpose(pst[:, slot * 128:(slot + 1) * 128],
                            xn_tmp[:, c * 128:(c + 1) * 128], identb[:]), waits=(w_xn, VC()))
                        vop(lambda v, slot=slot, c=c, t4=t4: v.tensor_copy(
                            xnT[:, c * 512 + t4 * 128: c * 512 + t4 * 128 + 128],
                            pst[:, slot * 128:(slot + 1) * 128]), waits=(TM(),))
                xnT_done = VC()

                for cc in range(4):
                    for dc in range(16):
                        top(lambda t, cc=cc, dc=dc: t.matmul(ps[cc][:, :],
                            WS1[:, dc * 512 + cc * 128: dc * 512 + cc * 128 + 128],
                            xnT[:, dc * 512:(dc + 1) * 512], start=(dc == 0), stop=(dc == 15)),
                            waits=(xnT_done, d_w0, BANKV[cc]) if dc == 0 else (), inc=(dc == 15))
                    w_g = TM()
                    vop(lambda v, cc=cc: v.tensor_copy(qcb[:, cc * 512:(cc + 1) * 512], ps[cc][:, :]), waits=(w_g,))
                    BANKV[cc] = VC()
                LRW1 = ("tM", CNT["tM"])
                qcb_done = VC()
                d_w1 = dma("dG", "gpsimd", WS2, WPA[:, 1, :], waits=(LRW2,))

                for h in range(16):
                    bk = h % 4
                    for cc in range(4):
                        top(lambda t, h=h, cc=cc, bk=bk: t.matmul(ps[bk][:, :],
                            WS2[:, cc * 2048 + h * 128: cc * 2048 + h * 128 + 128],
                            qcb[:, cc * 512:(cc + 1) * 512], start=(cc == 0), stop=(cc == 3)),
                            waits=(qcb_done, d_w1, BANKV[bk]) if cc == 0 else (), inc=(cc == 3))
                    w_q = TM()
                    vop(lambda v, bk=bk: v.tensor_copy(KRO[:], ps[bk][:, :]), waits=(w_q, d_cos, d_sin, SC()))
                    BANKV[bk] = VC()
                    vop(lambda v: v.tensor_copy(TC1[:], KRO[0:32, :]))
                    vop(lambda v: v.tensor_copy(TC2[:], KRO[32:64, :]), waits=(VC(),))
                    w_c12 = VC()
                    vop(lambda v: v.tensor_tensor(KRO[0:32, :], TC1[:], COSB[0:32, :], op=ALU.mult), waits=(w_c12,))
                    vop(lambda v: v.tensor_tensor(TRB[:], TC2[:], SINB[0:32, :], op=ALU.mult))
                    vop(lambda v: v.tensor_tensor(KRO[0:32, :], KRO[0:32, :], TRB[:], op=ALU.subtract), waits=(VC(),))
                    vop(lambda v: v.tensor_tensor(TRB[:], TC1[:], SINB[0:32, :], op=ALU.mult), waits=(VC(),))
                    vop(lambda v: v.tensor_tensor(TC1[:], TC2[:], COSB[0:32, :], op=ALU.mult))
                    vop(lambda v: v.tensor_tensor(TRB[:], TRB[:], TC1[:], op=ALU.add), waits=(VC(),))
                    vop(lambda v: v.tensor_copy(KRO[32:64, :], TRB[:]), waits=(VC(),))
                    rope_done = VC()
                    scop(lambda sc: sc.activation(SQ2[:], KRO[:], AF.Square), waits=(rope_done,))
                    top(lambda t: t.matmul(ps[4][0:1, 0:512], onescol[:, 0:1], SQ2[:], start=True, stop=True),
                        waits=(SC(), VC()))
                    w_ss = TM()
                    scop(lambda sc: sc.activation(QRS, ps[4][0:1, 0:512], AF.Ln, bias=sbc[0:1, 1:2], scale=1.0 / 128.0),
                         waits=(w_ss,))
                    scop(lambda sc: sc.activation(QRS, QRS, AF.Exp, bias=0.0, scale=-0.5), waits=(SC(),))
                    w_rs = SC()
                    top(lambda t: t.matmul(ps[5][:, :], onesrow[0:1, :], QRS, start=True, stop=True),
                        waits=(w_rs, VC()))
                    w_bc = TM()
                    vop(lambda v: v.tensor_tensor(SQ2[:], KRO[:], ps[5][:, :], op=ALU.mult), waits=(w_bc,))
                    sl = h % 2
                    wq_st = (LRQST[sl],) if LRQST[sl] else ()
                    vop(lambda v, sl=sl: v.tensor_scalar(qstS[sl][:, 0:512], SQ2[:], QNW, None, op0=ALU.mult),
                        waits=(VC(),) + wq_st)
                    w_ev = VC()
                    LRQST[sl] = dma("dG", "gpsimd", qTs[h][:, s * 512:(s + 1) * 512], qstS[sl][:, 0:512], waits=(w_ev,))
                LRW2 = ("tM", CNT["tM"])

                d_w2 = dma("dG", "gpsimd", WS1, WPA[:, 2, :], waits=(LRW1,))
                for gi, bnk in ((0, 0), (1, 1), (2, 2), (3, 3)):
                    for dc in range(16):
                        top(lambda t, gi=gi, bnk=bnk, dc=dc: t.matmul(ps[bnk][:, :],
                            WS1[:, gi * 2048 + dc * 128: gi * 2048 + dc * 128 + 128],
                            xnT[:, dc * 512:(dc + 1) * 512], start=(dc == 0), stop=(dc == 15)),
                            waits=(d_w2, BANKV[bnk]) if dc == 0 else (), inc=(dc == 15))
                LRW1 = ("tM", CNT["tM"])
                w_kv = TM()
                scop(lambda sc: sc.activation(SQ2[:], ps[0][:, :], AF.Square), waits=(w_kv, VC()))
                top(lambda t: t.matmul(ps[4][0:1, 0:512], onescol[:, 0:1], SQ2[:], start=True, stop=True),
                    waits=(SC(),))
                scop(lambda sc: sc.activation(QRS, ps[4][0:1, 0:512], AF.Ln, bias=sbc[0:1, 1:2], scale=1.0 / 128.0),
                     waits=(TM(),))
                scop(lambda sc: sc.activation(QRS, QRS, AF.Exp, bias=0.0, scale=-0.5), waits=(SC(),))
                top(lambda t: t.matmul(ps[5][:, :], onesrow[0:1, :], QRS, start=True, stop=True), waits=(SC(),))
                w_kbc = TM()
                vop(lambda v: v.tensor_copy(KRO[:], ps[0][:, :]), waits=(w_kbc,))
                vop(lambda v: v.tensor_tensor(KRO[:], KRO[:], ps[5][:, :], op=ALU.mult), waits=(VC(),))
                BANKV[0] = VC()
                vop(lambda v: v.tensor_scalar(KRO[:], KRO[:], KNW, None, op0=ALU.mult), waits=(VC(),))
                vop(lambda v: v.tensor_copy(TC1[:], KRO[0:32, :]), waits=(VC(),))
                vop(lambda v: v.tensor_copy(TC2[:], KRO[32:64, :]))
                w_c12 = VC()
                vop(lambda v: v.tensor_tensor(KRO[0:32, :], TC1[:], COSB[0:32, :], op=ALU.mult), waits=(w_c12,))
                vop(lambda v: v.tensor_tensor(TRB[:], TC2[:], SINB[0:32, :], op=ALU.mult))
                vop(lambda v: v.tensor_tensor(KRO[0:32, :], KRO[0:32, :], TRB[:], op=ALU.subtract), waits=(VC(),))
                vop(lambda v: v.tensor_tensor(TRB[:], TC1[:], SINB[0:32, :], op=ALU.mult), waits=(VC(),))
                vop(lambda v: v.tensor_tensor(TC1[:], TC2[:], COSB[0:32, :], op=ALU.mult))
                vop(lambda v: v.tensor_tensor(TRB[:], TRB[:], TC1[:], op=ALU.add), waits=(VC(),))
                vop(lambda v: v.tensor_copy(KRO[32:64, :], TRB[:]), waits=(VC(),))
                vop(lambda v, s=s: v.tensor_copy(kswT[:, 256 + s * 512: 256 + (s + 1) * 512], KRO[:]), waits=(VC(),))
                vop(lambda v: v.tensor_copy(vT_tmp[:, 0:512], ps[1][:, :]), waits=(w_kv,))
                BANKV[1] = VC()
                w_vt = VC()
                for k4 in range(4):
                    top(lambda t, k4=k4: t.transpose(pst[:, 0:128], vT_tmp[:, k4 * 128:(k4 + 1) * 128], identb[:]),
                        waits=(w_vt, VC()))
                    vop(lambda v, k4=k4, s=s: v.tensor_copy(
                        vsw[:, (2 + s * 4 + k4) * 128:(2 + s * 4 + k4 + 1) * 128], pst[:, 0:128]), waits=(TM(),))
                vop(lambda v: v.tensor_tensor(ZB[:], ps[3][:, :], BR1[:], op=ALU.add), waits=(w_kv, tab_done))
                BANKV[3] = VC()
                vop(lambda v: v.tensor_copy(CB[:], ps[2][:, :]), waits=(w_kv,))
                BANKV[2] = VC()
                w_zc = VC()
                z3 = ZB.rearrange("p (b m) -> p b m", m=16)
                c3 = CB.rearrange("p (b m) -> p b m", m=16)
                t13 = TR1.rearrange("p (b m) -> p b m", m=8)
                t23 = TR2.rearrange("p (b m) -> p b m", m=4)
                t33 = TR3.rearrange("p (b m) -> p b m", m=2)
                vop(lambda v: v.tensor_tensor(t13[:, :, :], z3[:, :, 0:8], z3[:, :, 8:16], op=ALU.max), waits=(w_zc,))
                vop(lambda v: v.tensor_tensor(t23[:, :, :], t13[:, :, 0:4], t13[:, :, 4:8], op=ALU.max), waits=(VC(),))
                vop(lambda v: v.tensor_tensor(t33[:, :, :], t23[:, :, 0:2], t23[:, :, 2:4], op=ALU.max), waits=(VC(),))
                vop(lambda v: v.tensor_tensor(ZMAX[:], t33[:, :, 0:1].rearrange("p b o -> p (b o)"),
                    t33[:, :, 1:2].rearrange("p b o -> p (b o)"), op=ALU.max), waits=(VC(),))
                w_zm = VC()
                for m in range(16):
                    vop(lambda v, m=m: v.tensor_tensor(z3[:, :, m:m+1].rearrange("p b o -> p (b o)"),
                        z3[:, :, m:m+1].rearrange("p b o -> p (b o)"), ZMAX[:], op=ALU.subtract),
                        waits=(w_zm,) if m == 0 else ())
                w_sub = VC()
                scop(lambda sc: sc.activation(ZB[:], ZB[:], AF.Exp), waits=(w_sub,))
                w_exp = SC()
                vop(lambda v: v.tensor_tensor(t13[:, :, :], z3[:, :, 0:8], z3[:, :, 8:16], op=ALU.add), waits=(w_exp,))
                vop(lambda v: v.tensor_tensor(t23[:, :, :], t13[:, :, 0:4], t13[:, :, 4:8], op=ALU.add), waits=(VC(),))
                vop(lambda v: v.tensor_tensor(t33[:, :, :], t23[:, :, 0:2], t23[:, :, 2:4], op=ALU.add), waits=(VC(),))
                vop(lambda v: v.tensor_tensor(ZSUM[:], t33[:, :, 0:1].rearrange("p b o -> p (b o)"),
                    t33[:, :, 1:2].rearrange("p b o -> p (b o)"), op=ALU.add), waits=(VC(),))
                vop(lambda v: v.reciprocal(RS[:], ZSUM[:]), waits=(VC(),))
                w_rs2 = VC()
                for m in range(16):
                    vop(lambda v, m=m: v.tensor_tensor(TR1[:, 0:32], z3[:, :, m:m+1].rearrange("p b o -> p (b o)"),
                        c3[:, :, m:m+1].rearrange("p b o -> p (b o)"), op=ALU.mult), waits=(VC(),))
                    if m == 0:
                        vop(lambda v: v.tensor_copy(ACCB[:], TR1[:, 0:32]), waits=(VC(),))
                    else:
                        vop(lambda v: v.tensor_tensor(ACCB[:], ACCB[:], TR1[:, 0:32], op=ALU.add), waits=(VC(),))
                vop(lambda v, s=s: v.tensor_tensor(CKVT[:, s * 32:(s + 1) * 32], ACCB[:], RS[:], op=ALU.mult),
                    waits=(w_rs2, VC()))

                d_w3 = dma("dG", "gpsimd", WS2, WPA[:, 3, :], waits=(LRW2,))
                for gi, bnk, wdt in ((0, 0, 64), (1, 1, 64), (2, 2, 64), (3, 3, 4)):
                    off = gi * 1024 if gi < 3 else 3072
                    mw = 64 if gi < 3 else 4
                    for dc in range(16):
                        top(lambda t, off=off, mw=mw, bnk=bnk, dc=dc: t.matmul(ps[bnk][0:mw, :],
                            WS2[:, off + dc * mw: off + (dc + 1) * mw],
                            xnT[:, dc * 512:(dc + 1) * 512], start=(dc == 0), stop=(dc == 15)),
                            waits=(d_w3, BANKV[bnk]) if dc == 0 else (), inc=(dc == 15))
                LRW2 = ("tM", CNT["tM"])
                w_ix = TM()
                vop(lambda v: v.tensor_tensor(ZB[0:64, :], ps[1][0:64, :], BR2[0:64, :], op=ALU.add), waits=(w_ix, VC()))
                BANKV[1] = VC()
                vop(lambda v: v.tensor_copy(CB[0:64, :], ps[0][0:64, :]), waits=(w_ix,))
                BANKV[0] = VC()
                w_zc = VC()
                vop(lambda v: v.tensor_tensor(t13[0:64, :, :], z3[0:64, :, 0:8], z3[0:64, :, 8:16], op=ALU.max), waits=(w_zc,))
                vop(lambda v: v.tensor_tensor(t23[0:64, :, :], t13[0:64, :, 0:4], t13[0:64, :, 4:8], op=ALU.max), waits=(VC(),))
                vop(lambda v: v.tensor_tensor(t33[0:64, :, :], t23[0:64, :, 0:2], t23[0:64, :, 2:4], op=ALU.max), waits=(VC(),))
                vop(lambda v: v.tensor_tensor(ZMAX[0:64, :], t33[0:64, :, 0:1].rearrange("p b o -> p (b o)"),
                    t33[0:64, :, 1:2].rearrange("p b o -> p (b o)"), op=ALU.max), waits=(VC(),))
                w_zm = VC()
                for m in range(16):
                    vop(lambda v, m=m: v.tensor_tensor(z3[0:64, :, m:m+1].rearrange("p b o -> p (b o)"),
                        z3[0:64, :, m:m+1].rearrange("p b o -> p (b o)"), ZMAX[0:64, :], op=ALU.subtract),
                        waits=(w_zm,) if m == 0 else ())
                w_sub = VC()
                scop(lambda sc: sc.activation(ZB[0:64, :], ZB[0:64, :], AF.Exp), waits=(w_sub,))
                w_exp = SC()
                vop(lambda v: v.tensor_tensor(t13[0:64, :, :], z3[0:64, :, 0:8], z3[0:64, :, 8:16], op=ALU.add), waits=(w_exp,))
                vop(lambda v: v.tensor_tensor(t23[0:64, :, :], t13[0:64, :, 0:4], t13[0:64, :, 4:8], op=ALU.add), waits=(VC(),))
                vop(lambda v: v.tensor_tensor(t33[0:64, :, :], t23[0:64, :, 0:2], t23[0:64, :, 2:4], op=ALU.add), waits=(VC(),))
                vop(lambda v: v.tensor_tensor(ZSUM[0:64, :], t33[0:64, :, 0:1].rearrange("p b o -> p (b o)"),
                    t33[0:64, :, 1:2].rearrange("p b o -> p (b o)"), op=ALU.add), waits=(VC(),))
                vop(lambda v: v.reciprocal(RS[0:64, :], ZSUM[0:64, :]), waits=(VC(),))
                w_rs2 = VC()
                for m in range(16):
                    vop(lambda v, m=m: v.tensor_tensor(TR1[0:64, 0:32], z3[0:64, :, m:m+1].rearrange("p b o -> p (b o)"),
                        c3[0:64, :, m:m+1].rearrange("p b o -> p (b o)"), op=ALU.mult), waits=(VC(),))
                    if m == 0:
                        vop(lambda v: v.tensor_copy(ACCB[0:64, :], TR1[0:64, 0:32]), waits=(VC(),))
                    else:
                        vop(lambda v: v.tensor_tensor(ACCB[0:64, :], ACCB[0:64, :], TR1[0:64, 0:32], op=ALU.add), waits=(VC(),))
                vop(lambda v: v.tensor_tensor(ACCB[0:64, :], ACCB[0:64, :], RS[0:64, :], op=ALU.mult), waits=(w_rs2, VC()))
                vop(lambda v, s=s: v.tensor_copy(kcompT[0:64, s * 32:(s + 1) * 32], ACCB[0:64, :]), waits=(VC(),))
                kcomp_done = VC()
                vop(lambda v: v.tensor_copy(qdb[0:64, 0:512], ps[2][0:64, :]), waits=(w_ix,))
                BANKV[2] = VC()
                qdb_done = VC()
                for ih in range(4):
                    top(lambda t, ih=ih: t.matmul(ps[4][0:64, :], PB3[0:64, ih * 64:(ih + 1) * 64],
                        qdb[0:64, 0:512], start=True, stop=True), waits=(qdb_done, VC()))
                    vop(lambda v, ih=ih: v.tensor_copy(qib[0:64, ih * 512:(ih + 1) * 512], ps[4][0:64, :]),
                        waits=(TM(),))
                qib_done = VC()
                vop(lambda v: v.tensor_copy(ZB[0:4, 0:512], ps[3][0:4, :]), waits=(w_ix,))
                BANKV[3] = VC()
                w_ww = VC()
                for ti in range(4):
                    top(lambda t, ti=ti: t.transpose(psl[:, 0:128], ZB[:, ti * 128:(ti + 1) * 128], identf[:]),
                        waits=(w_ww, VC()))
                    vop(lambda v, ti=ti: v.tensor_copy(WIH[:, ti * 4:(ti + 1) * 4], psl[:, 0:4]), waits=(TM(),))
                wih_done = VC()

                for ti in range(4):
                    gt = s * 4 + ti
                    d_cm = dma("dS", "sync", CMB, inp["cmask"][gt * 128:(gt + 1) * 128, :], waits=(VC(),))
                    d_cb = dma("dS", "sync", CBB, inp["cbias"][gt * 128:(gt + 1) * 128, :], waits=(VC(),))
                    for ih in range(4):
                        bk = ih % 2
                        top(lambda t, ih=ih, ti=ti, bk=bk: t.matmul(ps[bk][:, 0:NB],
                            qib[0:64, ih * 512 + ti * 128: ih * 512 + ti * 128 + 128],
                            kcompT[0:64, 0:NB], start=True, stop=True),
                            waits=(qib_done, kcomp_done, BANKV[bk]))
                        w_hs = TM()
                        vop(lambda v, bk=bk: v.tensor_scalar(TMPM[:, 0:NB], ps[bk][:, 0:NB], 0.0, None, op0=ALU.max),
                            waits=(w_hs, wih_done))
                        BANKV[bk] = VC()
                        if ih == 0:
                            vop(lambda v, ti=ti, ih=ih: v.tensor_scalar(ISC[:, 0:NB], TMPM[:, 0:NB],
                                WIH[:, ti * 4 + ih: ti * 4 + ih + 1], None, op0=ALU.mult), waits=(VC(),))
                        else:
                            vop(lambda v, ti=ti, ih=ih: v.tensor_scalar(TMPM[:, 0:NB], TMPM[:, 0:NB],
                                WIH[:, ti * 4 + ih: ti * 4 + ih + 1], None, op0=ALU.mult), waits=(VC(),))
                            vop(lambda v: v.tensor_tensor(ISC[:, 0:NB], ISC[:, 0:NB], TMPM[:, 0:NB], op=ALU.add),
                                waits=(VC(),))
                    vop(lambda v: v.tensor_tensor(ISC[:, 0:NB], ISC[:, 0:NB], CMB[:, 0:NB], op=ALU.mult),
                        waits=(d_cm, VC()))
                    vop(lambda v: v.tensor_tensor(ISC[:, 0:NB], ISC[:, 0:NB], CBB[:, 0:NB], op=ALU.add),
                        waits=(d_cb, VC()))
                    vop(lambda v: v.memset(LO, -1.0e6), waits=(VC(),))
                    vop(lambda v: v.memset(HIc, 1.0e6))
                    for it in range(BISECT):
                        vop(lambda v: v.tensor_tensor(MIDc, LO, HIc, op=ALU.add), waits=(VC(),))
                        vop(lambda v: v.tensor_scalar(MIDc, MIDc, 0.5, None, op0=ALU.mult), waits=(VC(),))
                        vop(lambda v: v.tensor_scalar(TMPM[:, 0:NB], ISC[:, 0:NB], MIDc, 0.0,
                            op0=ALU.is_gt, op1=ALU.add, accum_out=CNTC), waits=(VC(),))
                        vop(lambda v: v.tensor_scalar(FBL, CNTC, 64.0, None, op0=ALU.is_ge), waits=(VC(),))
                        vop(lambda v: v.tensor_tensor(D1, MIDc, LO, op=ALU.subtract), waits=(VC(),))
                        vop(lambda v: v.tensor_tensor(D1, D1, FBL, op=ALU.mult), waits=(VC(),))
                        vop(lambda v: v.tensor_tensor(LO, LO, D1, op=ALU.add), waits=(VC(),))
                        vop(lambda v: v.tensor_tensor(D2, HIc, MIDc, op=ALU.subtract))
                        vop(lambda v: v.tensor_tensor(D2, D2, FBL, op=ALU.mult), waits=(VC(),))
                        vop(lambda v: v.tensor_tensor(HIc, MIDc, D2, op=ALU.add), waits=(VC(),))
                    vop(lambda v: v.tensor_scalar(TMPM[:, 0:NB], ISC[:, 0:NB], LO, None, op0=ALU.is_gt), waits=(VC(),))
                    vop(lambda v: v.tensor_tensor(MSK[:, 0:NB], ISC[:, 0:NB], TMPM[:, 0:NB], op=ALU.mult), waits=(VC(),))
                    vop(lambda v: v.tensor_scalar(TMPM[:, 0:NB], TMPM[:, 0:NB], -1.0e30, 1.0e30,
                        op0=ALU.mult, op1=ALU.add), waits=(VC(),))
                    vop(lambda v: v.tensor_tensor(MSK[:, 0:NB], MSK[:, 0:NB], TMPM[:, 0:NB], op=ALU.add), waits=(VC(),))
                    vop(lambda v: v.tensor_reduce(V64, MSK[:, 0:NB], axis=AX.X, op=ALU.min), waits=(VC(),))
                    vop(lambda v: v.tensor_scalar(TMPM[:, 0:NB], ISC[:, 0:NB], V64, 0.0,
                        op0=ALU.is_gt, op1=ALU.add, accum_out=CNTG), waits=(VC(),))
                    vop(lambda v: v.tensor_scalar(MSK[:, 0:NB], ISC[:, 0:NB], V64, None, op0=ALU.is_equal), waits=(VC(),))
                    vop(lambda v: v.memset(ONE1[:, 0:NB], 1.0))
                    vop(lambda v: v.tensor_tensor_scan(TMPM[:, 0:NB], ONE1[:, 0:NB], MSK[:, 0:NB], 0.0,
                        op0=ALU.mult, op1=ALU.add), waits=(VC(),))
                    vop(lambda v: v.tensor_scalar(KPR, CNTG, -1.0, 64.0, op0=ALU.mult, op1=ALU.add), waits=(VC(),))
                    vop(lambda v: v.tensor_scalar(TMPM[:, 0:NB], TMPM[:, 0:NB], KPR, None, op0=ALU.is_le), waits=(VC(),))
                    vop(lambda v: v.tensor_tensor(MSK[:, 0:NB], MSK[:, 0:NB], TMPM[:, 0:NB], op=ALU.mult), waits=(VC(),))
                    vop(lambda v: v.tensor_scalar(TMPM[:, 0:NB], ISC[:, 0:NB], V64, None, op0=ALU.is_gt), waits=(VC(),))
                    vop(lambda v: v.tensor_tensor(MSK[:, 0:NB], MSK[:, 0:NB], TMPM[:, 0:NB], op=ALU.add), waits=(VC(),))
                    vop(lambda v: v.tensor_scalar(TMPM[:, 0:NB], ISC[:, 0:NB], -5.0e29, None, op0=ALU.is_gt), waits=(VC(),))
                    vop(lambda v: v.tensor_tensor(MSK[:, 0:NB], MSK[:, 0:NB], TMPM[:, 0:NB], op=ALU.mult), waits=(VC(),))
                    vop(lambda v: v.tensor_scalar(MSK[:, 0:NB], MSK[:, 0:NB], 1.0e30, -1.0e30,
                        op0=ALU.mult, op1=ALU.add), waits=(VC(),))
                    vop(lambda v: v.tensor_copy(mbb[:, 0:NB], MSK[:, 0:NB]), waits=(VC(),))
                    w_mbb = VC()
                    for c in range(NCH):
                        top(lambda t, c=c: t.transpose(pst[:, 0:128], mbb[:, c * 128:(c + 1) * 128], identb[:]),
                            waits=(w_mbb, VC()))
                        vop(lambda v, c=c, gt=gt: v.tensor_copy(MBT[:, c * T + gt * 128: c * T + (gt + 1) * 128],
                            pst[:, 0:128]), waits=(TM(),))

            scop(lambda sc: sc.activation(ZB[:, 0:256], CKVT[:], AF.Square), waits=(VC(),))
            top(lambda t: t.matmul(ps[4][0:1, 0:256], onescol[:, 0:1], ZB[:, 0:256], start=True, stop=True),
                waits=(SC(), VC()))
            scop(lambda sc: sc.activation(QRS[0:1, 0:256], ps[4][0:1, 0:256], AF.Ln, bias=sbc[0:1, 1:2], scale=1.0 / 128.0),
                 waits=(TM(),))
            scop(lambda sc: sc.activation(QRS[0:1, 0:256], QRS[0:1, 0:256], AF.Exp, bias=0.0, scale=-0.5), waits=(SC(),))
            top(lambda t: t.matmul(ps[5][:, 0:256], onesrow[0:1, :], QRS[0:1, 0:256], start=True, stop=True), waits=(SC(),))
            vop(lambda v: v.tensor_tensor(CB[:, 0:256], CKVT[:], ps[5][:, 0:256], op=ALU.mult), waits=(TM(),))
            vop(lambda v: v.tensor_scalar(CB[:, 0:256], CB[:, 0:256], KNW, None, op0=ALU.mult), waits=(VC(),))
            vop(lambda v: v.tensor_copy(ckvnT[:, 0:256], CB[:, 0:256]), waits=(VC(),))
            ckvn_done = VC()
            for c in range(NCH):
                top(lambda t, c=c: t.transpose(pst[:, 0:128], ckvnT[:, c * 128:(c + 1) * 128], identb[:]),
                    waits=(ckvn_done, VC()))
                vop(lambda v, c=c: v.tensor_copy(ckv_tok[:, c * 128:(c + 1) * 128], pst[:, 0:128]), waits=(TM(),))
            cktok_done = VC()
            scop(lambda sc: sc.activation(ZB[:, 0:256], CB[:, 0:256], AF.Square), waits=(ckvn_done,))
            top(lambda t: t.matmul(ps[4][0:1, 0:256], onescol[:, 0:1], ZB[:, 0:256], start=True, stop=True),
                waits=(SC(),))
            vop(lambda v: v.tensor_reduce(MAXC, ps[4][0:1, 0:256], axis=AX.X, op=ALU.max), waits=(TM(),))
            scop(lambda sc: sc.activation(LBSP, MAXC, AF.Ln, bias=sbc[0:1, 4:5], scale=1.0), waits=(VC(),))
            vop(lambda v: v.tensor_scalar(LBSP, LBSP, 0.5, float(np.log(SCALE)), op0=ALU.mult, op1=ALU.add),
                waits=(SC(),))
            vop(lambda v: v.memset(MAXC, 0.0), waits=(VC(),))
            NKW = (256 + T + 511) // 512
            for c in range(NKW):
                cw = min(512, 256 + T - c * 512)
                scop(lambda sc, c=c, cw=cw: sc.activation(SQ2[:, 0:cw], kswT[:, c * 512:c * 512 + cw], AF.Square),
                     waits=(VC(),))
                top(lambda t, cw=cw: t.matmul(ps[4][0:1, 0:cw], onescol[:, 0:1], SQ2[:, 0:cw], start=True, stop=True),
                    waits=(SC(), VC()))
                vop(lambda v, cw=cw: v.tensor_reduce(TMPC, ps[4][0:1, 0:cw], axis=AX.X, op=ALU.max), waits=(TM(),))
                vop(lambda v: v.tensor_tensor(MAXC, MAXC, TMPC, op=ALU.max), waits=(VC(),))
            scop(lambda sc: sc.activation(LBW, MAXC, AF.Ln, bias=sbc[0:1, 4:5], scale=1.0), waits=(VC(),))
            vop(lambda v: v.tensor_scalar(LBW, LBW, 0.5, float(np.log(SCALE)), op0=ALU.mult, op1=ALU.add),
                waits=(SC(),))
            lb_done = VC()

            d_sw = dma("dS", "sync", SWMB.rearrange("p (c f) -> p c f", f=256),
                       inp["swm"].rearrange("(c p) f -> p c f", p=128), waits=(VC(),))
            d_sw0 = dma("dS", "sync", SWM0B.rearrange("p (c f) -> p c f", f=256),
                        inp["swm0"].rearrange("(c p) f -> p c f", p=128), waits=(VC(),))
            LRQH = [None, None]
            LRPB = [("vC", 0), ("vC", 0)]
            d_at = []
            for s in range(NS):
                for h in range(16):
                    sl = h % 2
                    wq = (LRQH[sl],) if LRQH[sl] else (lb_done,)
                    d_qh = dma("dG", "gpsimd", qhbS[sl][:, 0:512], qTs[h][:, s * 512:(s + 1) * 512], waits=wq)
                    scop(lambda sc, sl=sl: sc.activation(SQ2[:], qhbS[sl][:, 0:512], AF.Square), waits=(d_qh, VC()))
                    top(lambda t: t.matmul(ps[4][0:1, 0:512], onescol[:, 0:1], SQ2[:], start=True, stop=True),
                        waits=(SC(), VC(), BANKV[4]))
                    w_q2 = TM()
                    scop(lambda sc: sc.activation(BSPR, ps[4][0:1, 0:512], AF.Ln, bias=sbc[0:1, 4:5], scale=1.0),
                         waits=(w_q2, lb_done))
                    w_ln = SC()
                    scop(lambda sc: sc.activation(BWR, BSPR, AF.Exp, bias=LBW, scale=0.5), waits=(w_ln,))
                    scop(lambda sc: sc.activation(BSPR, BSPR, AF.Exp, bias=LBSP, scale=0.5), waits=(SC(),))
                    w_b = SC()
                    top(lambda t: t.matmul(ps[4][:, :], onesrow[0:1, :], BSPR, start=True, stop=True), waits=(w_b,))
                    w_bc1 = TM()
                    vop(lambda v: v.tensor_scalar(RLA[:], ps[4][:, :], -1.0, None, op0=ALU.mult), waits=(w_bc1,))
                    nbsp_done = VC()
                    top(lambda t: t.matmul(ps[4][:, :], onesrow[0:1, :], BWR, start=True, stop=True),
                        waits=(nbsp_done,))
                    w_bc2 = TM()
                    vop(lambda v: v.tensor_scalar(RLB2[:], ps[4][:, :], -1.0, None, op0=ALU.mult), waits=(w_bc2,))
                    nbw_done = VC()
                    BANKV[4] = VC()
                    for c in range(NCH):
                        wch = min(128, NB - c * 128)
                        top(lambda t, c=c, wch=wch, sl=sl: t.matmul(ps[c][0:wch, :],
                            ckvnT[:, c * 128:c * 128 + wch], qhbS[sl][:, 0:512], start=True, stop=True),
                            waits=(d_qh, cktok_done, BANKV[c]))
                        w_sc1 = TM()
                        vop(lambda v, c=c, wch=wch: v.scalar_tensor_tensor(ZB[0:wch, 0:512], in0=ps[c][0:wch, :],
                            scalar=SCALE, in1=RLA[0:wch, :], op0=ALU.mult, op1=ALU.add),
                            waits=(w_sc1, nbsp_done, SC()))
                        BANKV[c] = VC()
                        vop(lambda v, c=c, wch=wch, s=s: v.tensor_tensor(ZB[0:wch, 0:512], ZB[0:wch, 0:512],
                            MBT[0:wch, c * T + s * 512: c * T + (s + 1) * 512], op=ALU.add), waits=(VC(),))
                        w_st = VC()
                        pbsl = c % 2
                        scop(lambda sc, pbsl=pbsl, wch=wch: sc.activation(pbS[pbsl][0:wch, 0:512], ZB[0:wch, 0:512],
                             AF.Exp), waits=(w_st, LRPB[pbsl]))
                        w_e = SC()
                        top(lambda t, c=c, wch=wch, pbsl=pbsl: t.matmul(psl[0:1, 0:512], onescolb[0:wch, 0:1],
                            pbS[pbsl][0:wch, 0:512], start=(c == 0), stop=(c == NCH - 1)), waits=(w_e,), inc=False)
                        top(lambda t, c=c, wch=wch, pbsl=pbsl: t.matmul(ps[2][:, :],
                            ckv_tok[0:wch, c * 128:(c + 1) * 128], pbS[pbsl][0:wch, 0:512],
                            start=(c == 0), stop=(c == NCH - 1)),
                            waits=(cktok_done, BANKV[2]) if c == 0 else ())
                        LRPB[pbsl] = ("tM", CNT["tM"])
                    w_pv = TM()
                    for b2 in range(2):
                        n = s * 2 + b2
                        swsrc = SWM0B if n == 0 else SWMB
                        for c4 in range(4):
                            bw = c4 % 2
                            top(lambda t, n=n, c4=c4, bw=bw, b2=b2, sl=sl: t.matmul(
                                ps[bw][:, (c4 // 2) * 256:(c4 // 2) * 256 + 256],
                                kswT[:, (2 * n + c4) * 128:(2 * n + c4 + 1) * 128],
                                qhbS[sl][:, b2 * 256:(b2 + 1) * 256], start=True, stop=True),
                                waits=(BANKV[bw], d_sw, d_sw0) if c4 < 2 else ())
                            w_ws = TM()
                            vop(lambda v, c4=c4, bw=bw, swsrc=swsrc: v.scalar_tensor_tensor(
                                ZB[:, 0:256], in0=ps[bw][:, (c4 // 2) * 256:(c4 // 2) * 256 + 256],
                                scalar=SCALE, in1=swsrc[:, c4 * 256:(c4 + 1) * 256], op0=ALU.mult, op1=ALU.add),
                                waits=(w_ws, nbw_done, SC()))
                            if c4 >= 2:
                                BANKV[bw] = VC()
                            vop(lambda v, b2=b2: v.tensor_tensor(ZB[:, 0:256], ZB[:, 0:256],
                                RLB2[:, b2 * 256:(b2 + 1) * 256], op=ALU.add), waits=(VC(),))
                            w_st = VC()
                            pbsl2 = c4 % 2
                            scop(lambda sc, pbsl2=pbsl2: sc.activation(pbS[pbsl2][:, 0:256], ZB[:, 0:256], AF.Exp),
                                 waits=(w_st, LRPB[pbsl2]))
                            w_e = SC()
                            top(lambda t, b2=b2, c4=c4, pbsl2=pbsl2: t.matmul(ps[5][0:1, b2 * 256:(b2 + 1) * 256],
                                onescolb[:, 0:1], pbS[pbsl2][:, 0:256], start=(c4 == 0), stop=(c4 == 3)),
                                waits=(w_e, BANKV[5]) if (b2 == 0 and c4 == 0) else (w_e,), inc=False)
                            top(lambda t, n=n, c4=c4, b2=b2, pbsl2=pbsl2: t.matmul(
                                ps[3][:, b2 * 256:(b2 + 1) * 256],
                                vsw[:, (2 * n + c4) * 128:(2 * n + c4 + 1) * 128],
                                pbS[pbsl2][:, 0:256], start=(c4 == 0), stop=(c4 == 3)),
                                waits=(BANKV[3],) if (b2 == 0 and c4 == 0) else ())
                            LRPB[pbsl2] = ("tM", CNT["tM"])
                    w_wpv = TM()
                    LRQH[sl] = ("tM", CNT["tM"])
                    vop(lambda v: v.tensor_scalar(QRS, psl[0:1, 0:512], 1e-30, None, op0=ALU.add), waits=(w_pv, w_wpv))
                    vop(lambda v: v.reciprocal(QRS, QRS), waits=(VC(),))
                    w_rl1 = VC()
                    top(lambda t: t.matmul(ps[4][:, :], onesrow[0:1, :], QRS, start=True, stop=True),
                        waits=(w_rl1, BANKV[4]))
                    w_rb1 = TM()
                    vop(lambda v: v.tensor_copy(RLA[:], ps[4][:, :]), waits=(w_rb1,))
                    w_rla = VC()
                    vop(lambda v: v.tensor_scalar(QRS, ps[5][0:1, 0:512], 1e-30, None, op0=ALU.add), waits=(w_wpv, w_rla))
                    vop(lambda v: v.reciprocal(QRS, QRS), waits=(VC(),))
                    w_rl2 = VC()
                    top(lambda t: t.matmul(ps[4][:, :], onesrow[0:1, :], QRS, start=True, stop=True),
                        waits=(w_rl2, VC()))
                    w_rb2 = TM()
                    BANKV[5] = ("tM", CNT["tM"])
                    vop(lambda v: v.tensor_copy(RLB2[:], ps[4][:, :]), waits=(w_rb2,))
                    BANKV[4] = VC()
                    w_rlb = VC()
                    vop(lambda v: v.tensor_tensor(AT1[:], ps[2][:, :], RLA[:], op=ALU.mult), waits=(w_rlb,))
                    BANKV[2] = VC()
                    vop(lambda v: v.tensor_tensor(AT2[:], ps[3][:, :], RLB2[:], op=ALU.mult), waits=(VC(),))
                    BANKV[3] = VC()
                    vop(lambda v, h=h: v.tensor_tensor(WS2[:, h * 512:(h + 1) * 512], AT1[:], AT2[:], op=ALU.add),
                        waits=(VC(),))
                    w_at = VC()
                for h in range(16):
                    d_at.append(dma("dG", "gpsimd", ATs[:, (s * 16 + h) * 512:(s * 16 + h + 1) * 512],
                                    WS2[:, h * 512:(h + 1) * 512], waits=(w_at,)))

            d_gw = dma("dG", "gpsimd", WS1, WGRPs.rearrange("p g c f -> p (g c f)"), waits=(LRW1,) + tuple(d_at))
            LRAT = None
            LRWF = None
            LROT = ("vC", 0)
            LRXR = [None, None]
            LRYS = [None, None]
            for s in range(NS):
                wts = (d_gw, d_at[-1]) if LRAT is None else (d_gw, LRAT)
                if LRWF is not None:
                    wts = wts + (LRWF,)
                d_a = dma("dG", "gpsimd", WS2, ATs[:, s * 8192:(s + 1) * 8192], waits=wts)
                for gch in range(16):
                    g_ = gch // 4
                    bk = gch % 4
                    for dc in range(4):
                        top(lambda t, g_=g_, gch=gch, dc=dc, bk=bk: t.matmul(ps[bk][:, :],
                            WS1[:, g_ * 2048 + dc * 512 + (gch % 4) * 128: g_ * 2048 + dc * 512 + (gch % 4) * 128 + 128],
                            WS2[:, (g_ * 4 + dc) * 512:(g_ * 4 + dc + 1) * 512], start=(dc == 0), stop=(dc == 3)),
                            waits=(d_a, BANKV[bk], LROT) if dc == 0 else (), inc=(dc == 3))
                    w_g = TM()
                    vop(lambda v, gch=gch, bk=bk: v.tensor_copy(oT[:, gch * 512:(gch + 1) * 512], ps[bk][:, :]),
                        waits=(w_g,))
                    BANKV[bk] = VC()
                LRAT = ("tM", CNT["tM"])
                oT_done = VC()
                for os_ in range(4):
                    wwf = (oT_done, LRAT) if LRWF is None else (oT_done, LRAT, LRWF)
                    d_wf = dma("dG", "gpsimd", WS2[:, 0:8192].rearrange("p (c f) -> p c f", c=16),
                               WFINs[:, :, os_ * 512:(os_ + 1) * 512].rearrange("c p f -> p c f"),
                               waits=wwf)
                    for tsub in range(4):
                        bk = tsub % 2
                        xsl = tsub % 2
                        wxr = (VC(),) if LRXR[xsl] is None else (LRXR[xsl],)
                        d_xr = dma("dS", "sync", XRES[xsl],
                                   src3[s * 4 + tsub][:, os_ * 512:(os_ + 1) * 512], waits=wxr)
                        for kc in range(16):
                            top(lambda t, kc=kc, tsub=tsub, bk=bk: t.matmul(ps[bk][:, :],
                                oT[:, kc * 512 + tsub * 128: kc * 512 + tsub * 128 + 128],
                                WS2[:, kc * 512:(kc + 1) * 512], start=(kc == 0), stop=(kc == 15)),
                                waits=(d_wf, oT_done, BANKV[bk]) if kc == 0 else (), inc=(kc == 15))
                        w_f = TM()
                        ysl_ = tsub % 2
                        wys = (LRYS[ysl_],) if LRYS[ysl_] else ()
                        vop(lambda v, bk=bk, xsl=xsl, ysl_=ysl_: v.tensor_tensor(YSL[ysl_][:], ps[bk][:, :],
                            XRES[xsl][:], op=ALU.add), waits=(w_f, d_xr) + wys)
                        BANKV[bk] = VC()
                        LRXR[xsl] = VC()
                        w_y = VC()
                        LRYS[ysl_] = dma("dG", "gpsimd", dst3[s * 4 + tsub][:, os_ * 512:(os_ + 1) * 512],
                                         YSL[ysl_][:], waits=(w_y,))
                    LRWF = ("tM", CNT["tM"])
                LROT = ("tM", CNT["tM"])

            if Q["dG"] % 2 == 1:
                dma("dG", "gpsimd", qiTs[0][:, 0:2], sbc[:, 0:2], waits=())
            if Q["dS"] % 2 == 1:
                dma("dS", "sync", dumS, sbc[:, 0:1], waits=())

            for ek, engname in (("sync", "sync"), ("vector", "vector"), ("scalar", "scalar"),
                                ("tensor", "tensor"), ("gpsimd", "gpsimd")):
                lst = OPS[ek]
                def runner(e, lst=lst):
                    sat = {}
                    for waits, fn in lst:
                        for sname, val in waits:
                            if val > 0 and sat.get(sname, 0) < val:
                                e.wait_ge(sem[sname], val)
                                sat[sname] = val
                        fn(e)
                ew(engname, runner)
            add("tM", CNT["tM"] - get("tM"))
            add("vC", CNT["vC"] - get("vC"))
            add("sC", CNT["sC"] - get("sC"))
            marks["last_texM"] = get("tM")

        def emit_attnb(src, dst):
            xnT = PB2[:, 0:8192]
            xtile = PF2[:, 0:4096]
            sqb = PF2[:, 4096:6144]
            xn_tmp = PB3[:, 4096:6144]
            lnt = SM[:, 128:160]
            kTb = PE1[:, 0:4096]
            vbuf = PE1[:, 4096:8192]
            qst = PB1[:, 0:8192]
            vT_tmp = PB1[:, 8192:8704]
            wqbuf = [PB1[:, 8704:10752], PB1[:, 10752:12800]]
            WkB = PB3[:, 6144:8192]
            WvB = PB3[:, 8192:10240]
            bqT = SM[:, 232:248]
            bkT = SM[:, 248:249]
            bvT = SM[:, 249:250]
            maxk2 = SM[0:1, 250:251]
            tmpk2 = SM[0:1, 251:252]
            Lb = SM[0:1, 252:253]
            k2 = PF2[:, 4096:5120]

            src3 = src.rearrange("(n p) d -> n p d", p=128)
            dst3 = dst.rearrange("(n p) d -> n p d", p=128)
            xs4 = src3.rearrange("(s u) p d -> s u p d", u=4)
            ds4 = dst3.rearrange("(s u) p d -> s u p d", u=4)
            kTb3 = kTb.rearrange("p (S f) -> p S f", f=512)
            vbuf3 = vbuf.rearrange("p (S f) -> p S f", f=512)
            qst3 = qst.rearrange("p (o f) -> p o f", f=512)

            VC_A, VS_A, SC_A, SS_A, TM_A, DG_A = 96, 20, 9, 4, 87, 36
            bVC = get("vC") + 1
            bSC = get("sC"); bTM = get("tM"); bVS = get("vS"); bSS = get("sS")
            baseG = Q["dG"]
            assert baseG % 2 == 0
            bG2 = baseG // 2
            m_prevtM = marks.get("last_texM", 0)

            def a_gp(g):
                for s in range(NS):
                    sV = s * VC_A; sC_ = s * SC_A; sT = s * TM_A; sQ = s * (DG_A // 2)
                    for t in range(4):
                        pos = t
                        subx = "dG" + str((baseG + pos) % 2)
                        off = (18 * (t - 2) + 2) if t >= 2 else (18 * (t + 2) + 2 - VC_A)
                        offs = (2 * (t - 2) + 1) if t >= 2 else (2 * (t + 2) + 1 - SC_A)
                        wrg(g, None, "vC", None, None, bVC + sV + off, clamp=bVC)
                        wrg(g, None, "sC", None, None, bSC + sC_ + offs, clamp=bSC)
                        wrg(g, None, subx, None, None, 16 * (bG2 + sQ + pos // 2), clamp=None)
                        g.dma_start(out=xtile[:, (t % 2) * 2048:(t % 2) * 2048 + 2048],
                                    in_=xs4[s, t]).then_inc(sem[subx], 16)
                    for och in range(16):
                        pos = 4 + och
                        subw = "dG" + str((baseG + pos) % 2)
                        wrg(g, None, "tM", None, None, bTM + sT + 72 + och - 2, clamp=bTM)
                        wrg(g, None, subw, None, None, 16 * (bG2 + sQ + pos // 2), clamp=None)
                        g.dma_start(out=wqbuf[och % 2][:, 0:2048].rearrange("p (c f) -> p c f", c=16),
                                    in_=WQs[och]).then_inc(sem[subw], 16)
                    for och in range(16):
                        pos = 20 + och
                        subq = "dG" + str((baseG + pos) % 2)
                        wrg(g, None, "vC", None, None, bVC + sV + 81 + och, clamp=None)
                        wrg(g, None, subq, None, None, 16 * (bG2 + sQ + pos // 2), clamp=None)
                        g.dma_start(out=qTs[och][:, bass.ts(s, 512)],
                                    in_=qst3[:, och, :]).then_inc(sem[subq], 16)
            ew("gpsimd", a_gp)
            Q["dG"] += NS * DG_A

            def a_sc(sc):
                for s in range(NS):
                    sV = s * VC_A; sS4 = s * SS_A; sT = s * TM_A; sQ = s * (DG_A // 2)
                    for t in range(4):
                        subx = "dG" + str((baseG + t) % 2)
                        off = (18 * (t - 1) + 1) if t >= 1 else (18 * 3 + 1 - VC_A)
                        wrg(sc, None, "vC", None, None, bVC + sV + off, clamp=bVC)
                        if t == 0:
                            wrg(sc, None, "tM", None, None, bTM + sT + 71 - TM_A, clamp=bTM)
                        wrg(sc, None, subx, None, None, 16 * (bG2 + sQ + t // 2 + 1), clamp=None)
                        sc.activation(sqb[:, 0:2048], xtile[:, (t % 2) * 2048:(t % 2) * 2048 + 2048],
                                      AF.Square).then_inc(sem["sC"], 1)
                        wrg(sc, None, "vC", None, None, bVC + sV + 18 * t + 1, clamp=None)
                        sc.activation(lnt[:, 6:7], lnt[:, 5:6], AF.Ln,
                                      bias=sbc[:, 0:1], scale=1.0).then_inc(sem["sS"], 1)
                        wrg(sc, None, "sS", None, None, bSS + sS4 + t + 1, clamp=None)
                        sc.activation(lnt[:, 7:8], lnt[:, 6:7], AF.Exp,
                                      bias=0.0, scale=-0.5).then_inc(sem["sC"], 1)
                    wrg(sc, None, "vC", None, None, bVC + sV + 73, clamp=None)
                    sc.activation(k2[:, 0:512], kTb3[:, s, :],
                                  AF.Square).then_inc(sem["sC"], 1)
            ew("scalar", a_sc)

            def a_ve(v):
                v.memset(maxk2, 0.0).then_inc(sem["vC"], 1)
                for s in range(NS):
                    sV = s * VC_A; sS = s * VS_A; sC_ = s * SC_A; sT = s * TM_A
                    sQ = s * (DG_A // 2)
                    for t in range(4):
                        subx = "dG" + str((baseG + t) % 2)
                        wrg(v, None, subx, None, None, 16 * (bG2 + sQ + t // 2 + 1), clamp=None)
                        wrg(v, None, "vS", None, None, bVS + sS + 5 * t - 5 + 5, clamp=bVS)
                        v.tensor_reduce(lnt[:, 0:1], xtile[:, (t % 2) * 2048:(t % 2) * 2048 + 2048],
                                        axis=AX.X, op=ALU.add).then_inc(sem["vS"], 1)
                        wrg(v, None, "vS", None, None, bVS + sS + 5 * t + 1, clamp=None)
                        v.tensor_scalar(lnt[:, 1:2], lnt[:, 0:1], -1.0 / D, None, op0=ALU.mult).then_inc(sem["vS"], 1)
                        wrg(v, None, "sC", None, None, bSC + sC_ + 2 * t + 1, clamp=None)
                        v.tensor_reduce(lnt[:, 2:3], sqb[:, 0:2048], axis=AX.X, op=ALU.add).then_inc(sem["vS"], 1)
                        wrg(v, None, "vS", None, None, bVS + sS + 5 * t + 3, clamp=None)
                        v.tensor_scalar(lnt[:, 3:4], lnt[:, 2:3], 1.0 / D, None, op0=ALU.mult).then_inc(sem["vS"], 1)
                        v.tensor_tensor(lnt[:, 4:5], lnt[:, 1:2], lnt[:, 1:2], op=ALU.mult).then_inc(sem["vS"], 1)
                        wrg(v, None, "vS", None, None, bVS + sS + 5 * t + 5, clamp=None)
                        v.tensor_tensor(lnt[:, 5:6], lnt[:, 3:4], lnt[:, 4:5],
                                        op=ALU.subtract).then_inc(sem["vC"], 1)
                        wrg(v, None, "sC", None, None, bSC + sC_ + 2 * t + 2, clamp=None)
                        offx = (16 * (t - 1) + 16) if t >= 1 else (16 * 3 + 16 - TM_A)
                        wrg(v, None, "tM", None, None, bTM + sT + offx, clamp=max(bTM, m_prevtM))
                        v.tensor_scalar(xn_tmp[:], xtile[:, (t % 2) * 2048:(t % 2) * 2048 + 2048],
                                        lnt[:, 1:2], lnt[:, 7:8], op0=ALU.add, op1=ALU.mult).then_inc(sem["vC"], 1)
                        for c in range(16):
                            slot = c % 4
                            wrg(v, None, "tM", None, None, bTM + sT + 16 * t + c + 1, clamp=None)
                            v.tensor_copy(xnT[:, c * 512 + t * 128: c * 512 + t * 128 + 128],
                                          pst[:, slot * 128:(slot + 1) * 128]).then_inc(sem["vC"], 1)
                    wrg(v, None, "tM", None, None, bTM + sT + 65, clamp=None)
                    v.tensor_scalar(kTb3[:, s, :],
                                    ps[0][:, :], bkT, None, op0=ALU.add).then_inc(sem["vC"], 1)
                    wrg(v, None, "tM", None, None, bTM + sT + 66, clamp=None)
                    v.tensor_scalar(vT_tmp[:, 0:512], ps[1][:, :], bvT, None, op0=ALU.add).then_inc(sem["vC"], 1)
                    for k in range(4):
                        wrg(v, None, "tM", None, None, bTM + sT + 67 + k, clamp=None)
                        v.tensor_copy(vbuf3[:, s, k * 128:(k + 1) * 128],
                                      pst[:, 0:128]).then_inc(sem["vC"], 1)
                    wrg(v, None, "tM", None, None, bTM + sT + 71, clamp=None)
                    v.tensor_reduce(tmpk2, psl[0:1, 0:512], axis=AX.X, op=ALU.max).then_inc(sem["vC"], 1)
                    wrg(v, None, "vC", None, None, bVC + sV + 79, clamp=None)
                    v.tensor_tensor(maxk2, maxk2, tmpk2, op=ALU.max).then_inc(sem["vC"], 1)
                    for och in range(16):
                        wrg(v, None, "tM", None, None, bTM + sT + 72 + och, clamp=None)
                        wrg(v, None, "dG" + str((baseG + 20 + och) % 2), None, None,
                            16 * (bG2 + sQ + (20 + och) // 2 + 1 - DG_A // 2), clamp=0)
                        v.tensor_scalar(qst3[:, och, :], ps[2 + och % 2][:, :],
                                        bqT[:, och:och + 1], None, op0=ALU.add).then_inc(sem["vC"], 1)
            ew("vector", a_ve)

            def a_te(t_):
                for s in range(NS):
                    sV = s * VC_A; sC_ = s * SC_A; sT = s * TM_A; sQ = s * (DG_A // 2)
                    for t in range(4):
                        for c in range(16):
                            slot = c % 4
                            wrg(t_, None, "vC", None, None, bVC + sV + 18 * t + 2 + c, clamp=None)
                            t_.transpose(pst[:, slot * 128:(slot + 1) * 128],
                                         xn_tmp[:, c * 128:(c + 1) * 128], identb[:]).then_inc(sem["tM"], 1)
                    wrg(t_, None, "vC", None, None, bVC + sV + 72, clamp=None)
                    for dc in range(16):
                        ins = t_.matmul(ps[0][:, :], WkB[:, dc * 128:(dc + 1) * 128],
                                        xnT[:, dc * 512:(dc + 1) * 512],
                                        start=(dc == 0), stop=(dc == 15))
                    ins.then_inc(sem["tM"], 1)
                    for dc in range(16):
                        ins = t_.matmul(ps[1][:, :], WvB[:, dc * 128:(dc + 1) * 128],
                                        xnT[:, dc * 512:(dc + 1) * 512],
                                        start=(dc == 0), stop=(dc == 15))
                    ins.then_inc(sem["tM"], 1)
                    for k in range(4):
                        wrg(t_, None, "vC", None, None, bVC + sV + 74 + k, clamp=None)
                        t_.transpose(pst[:, 0:128], vT_tmp[:, k * 128:(k + 1) * 128],
                                     identb[:]).then_inc(sem["tM"], 1)
                    wrg(t_, None, "sC", None, None, bSC + sC_ + 9, clamp=None)
                    t_.matmul(psl[0:1, 0:512], onescol[:, 0:1], k2[:, 0:512],
                              start=True, stop=True).then_inc(sem["tM"], 1)
                    for och in range(16):
                        subw = "dG" + str((baseG + 4 + och) % 2)
                        wrg(t_, None, subw, None, None, 16 * (bG2 + sQ + (4 + och) // 2 + 1), clamp=None)
                        wrg(t_, None, "vC", None, None, bVC + sV + 81 + och - 2, clamp=bVC)
                        buf = wqbuf[och % 2]
                        for dc in range(16):
                            ins = t_.matmul(ps[2 + och % 2][:, :], buf[:, dc * 128:(dc + 1) * 128],
                                            xnT[:, dc * 512:(dc + 1) * 512],
                                            start=(dc == 0), stop=(dc == 15))
                        ins.then_inc(sem["tM"], 1)
            ew("tensor", a_te)

            add("vC", 1 + NS * VC_A); add("sC", NS * SC_A); add("tM", NS * TM_A)
            add("vS", NS * VS_A); add("sS", NS * SS_A)
            marks["last_texM"] = get("tM")

            c_ve_done = get("vC")
            def e_sc(sc, c_ve_done=c_ve_done):
                sc.wait_ge(sem["vC"], c_ve_done)
                sc.activation(Lb, maxk2, AF.Ln, bias=sbc[0:1, 4:5], scale=1.0).then_inc(sem["sC"], 1)
            ew("scalar", e_sc)
            c_lb1 = add("sC", 1)

            def e_ve(v, c_lb1=c_lb1):
                v.wait_ge(sem["sC"], c_lb1)
                v.tensor_scalar(Lb, Lb, 0.5, float(np.log(SCALE)), op0=ALU.mult, op1=ALU.add).then_inc(sem["vC"], 1)
            ew("vector", e_ve)
            add("vC", 1)

            def _p3b():
                attn_bT = PB1[:, 0:8192]
                attn_bTp = [PB1[:, 0:8192], PB1[:, 512:8704]]
                wobuf = [PB1[:, 8192:16384], PB1[:, 16384:24576]]
                qTh = [PB2[:, 0:512], PB2[:, 512:1024]]
                pTb = [PB2[:, 1024:1536], PB2[:, 1536:2048]]
                q2b = [PF2[:, 4096:4608], PF2[:, 4608:5120]]
                x2sl = [PF2[:, 0:512], PF2[:, 512:1024]]
                nbb = [PF1[:, 0:512], PF1[:, 512:1024]]
                rlb = [PF1[:, 1024:1536], PF1[:, 1536:2048]]
                x3st = [PF1[:, 2048:2560], PF1[:, 2560:3072]]
                tmpf = [PF1[:, 4096:4608], PF1[:, 4608:5120]]
                brow = SM[0:1, 256:768]

                NKC = T // 128
                TM_H = 2 * NKC + 3
                SC_H = NKC + 3
                TM_B, VC_B, SY_B, SC_B = 16 * TM_H + 16, 96, 16 * NKC, 16 * SC_H
                DG_B = 52
                QSEQ = [("wo", 0), ("wo", 1), ("x2", 0), ("x2", 1)]
                for g_ in range(16):
                    if g_ == 6:
                        QSEQ.append(("wo", 2))
                    if g_ == 10:
                        QSEQ.append(("wo", 3))
                    if g_ + 2 < 16:
                        QSEQ.append(("x2", g_ + 2))
                    QSEQ.append(("st", g_))
                POS_X2 = {a: 16 + i for i, (k_, a) in enumerate(QSEQ) if k_ == "x2"}
                POS_WO = {a: 16 + i for i, (k_, a) in enumerate(QSEQ) if k_ == "wo"}
                POS_ST = {a: 16 + i for i, (k_, a) in enumerate(QSEQ) if k_ == "st"}
                bVC = get("vC"); bSC = get("sC"); bTM = get("tM"); bSY = get("sY")
                baseG = Q["dG"]
                assert baseG % 2 == 0
                bG2 = baseG // 2
                QSC = 16 * (DG_B // 2)

                def sco_pos(kc):
                    return kc + 3 if kc < 2 else 2 * kc + 1

                def lpv_pos(kc):
                    return 2 * kc + 6 if kc <= NKC - 3 else NKC + 3 + kc

                def b_gp(g):
                    for s in range(NS):
                        sV = s * VC_B; sT = s * TM_B
                        sQ = s * (DG_B // 2)
                        for j in range(H):
                            par = j % 2
                            sub = "dG" + str((baseG + par) % 2)
                            wrg(g, None, "tM", None, None, bTM + sT + (j - 1) * TM_H, clamp=bTM)
                            wrg(g, None, sub, None, None, 16 * (bG2 + sQ + j // 2), clamp=None)
                            g.dma_start(out=qTh[par][:, 0:512],
                                        in_=qTs[j][:, s * 512:(s + 1) * 512]).then_inc(sem[sub], 16)
                        for idx, (kind, a) in enumerate(QSEQ):
                            pos = 16 + idx
                            sub = "dG" + str((baseG + pos) % 2)
                            if kind == "wo":
                                wrg(g, None, "tM", None, None, bTM + sT + 16 * TM_H + 4 * (a - 2) + 4, clamp=bTM)
                                wrg(g, None, sub, None, None, 16 * (bG2 + sQ + pos // 2), clamp=None)
                                g.dma_start(out=wobuf[a % 2][:, 0:8192].rearrange("p (c f) -> p c f", c=16),
                                            in_=WOs[a].rearrange("c p f -> p c f")).then_inc(sem[sub], 16)
                            elif kind == "x2":
                                wrg(g, None, "vC", None, None, bVC + sV + 80 + a - 1, clamp=bVC)
                                wrg(g, None, sub, None, None, 16 * (bG2 + sQ + pos // 2), clamp=None)
                                dq_, t_i = a // 4, a % 4
                                g.dma_start(out=x2sl[a % 2][:, 0:512],
                                            in_=xs4[s, t_i][:, dq_ * 512:(dq_ + 1) * 512]).then_inc(sem[sub], 16)
                            else:
                                wrg(g, None, "vC", None, None, bVC + sV + 80 + a + 1, clamp=None)
                                wrg(g, None, sub, None, None, 16 * (bG2 + sQ + pos // 2), clamp=None)
                                dq_, t_i = a // 4, a % 4
                                g.dma_start(out=ds4[s, t_i][:, dq_ * 512:(dq_ + 1) * 512],
                                            in_=x3st[a % 2][:, 0:512]).then_inc(sem[sub], 16)
                ew("gpsimd", b_gp)
                Q["dG"] += NS * DG_B

                def b_sc(sc):
                    rt = wregr(sc, "sc_rt")
                    rsC = wregr(sc, "sc_rsC"); rsT = wregr(sc, "sc_rsT")
                    rsY = wregr(sc, "sc_rsY"); rsQ = wregr(sc, "sc_rsQ")
                    with nc.Fori(0, NS, engines=[ET.Activation]) as s_:
                        s = urv(s_, NS - 1)
                        for r, k in [(rsC, SC_B), (rsT, TM_B), (rsY, SY_B), (rsQ, QSC)]:
                            sc.reg_mov(r, s)
                            sc.reg_mul(r, r, k)
                        def sc_inner3(sc):
                            with nc.Fori(0, H, step=2, engines=[ET.Activation]) as h2_:
                                h2 = urv(h2_, H - 2)
                                for par in range(2):
                                    pT = par * TM_H; pC = par * SC_H; pY = par * NKC
                                    sub = "dG" + str((baseG + par) % 2)
                                    wrg(sc, rt, sub, rsQ, (h2, 8), 16 * bG2 + 16, clamp=None)
                                    wrg(sc, rt, "tM", rsT, (h2, TM_H), bTM + pT + 1 - 2 * TM_H, clamp=bTM)
                                    sc.activation(q2b[par][:, 0:512], qTh[par][:, 0:512],
                                                  AF.Square).then_inc(sem["sC"], 1)
                                    wrg(sc, rt, "tM", rsT, (h2, TM_H), bTM + pT + 1, clamp=None)
                                    sc.activation(brow, ps[4][0:1, 0:512], AF.Ln,
                                                  bias=sbc[0:1, 4:5], scale=1.0).then_inc(sem["sC"], 1)
                                    wrg(sc, rt, "sC", rsC, (h2, SC_H), bSC + pC + 2, clamp=None)
                                    sc.activation(brow, brow, AF.Exp, bias=Lb, scale=0.5).then_inc(sem["sC"], 1)
                                    for kc in range(NKC):
                                        wrg(sc, rt, "sY", rsY, (h2, NKC), bSY + pY + kc + 1, clamp=None)
                                        if kc < 2:
                                            wrg(sc, rt, "tM", rsT, (h2, TM_H), bTM + pT + TM_H - 1 + kc - TM_H, clamp=bTM)
                                        else:
                                            wrg(sc, rt, "tM", rsT, (h2, TM_H), bTM + pT + 2 * kc + 2, clamp=None)
                                        sc.activation(pTb[kc % 2][:, 0:512], tmpf[kc % 2][:, 0:512],
                                                      AF.Exp).then_inc(sem["sC"], 1)
                            free_snap(h2_)
                        sc_inner3(sc)
                    free_snap(s_)
                ew("scalar", b_sc)

                def b_ve(v):
                    rt = wregr(v, "ve_rt")
                    rsV = wregr(v, "ve_rsV"); rsC = wregr(v, "ve_rsC"); rsT = wregr(v, "ve_rsT")
                    rsY = wregr(v, "ve_rsY"); rsQ = wregr(v, "ve_rsQ")
                    with nc.Fori(0, NS, engines=[ET.DVE]) as s_:
                        s = urv(s_, NS - 1)
                        for r, k in [(rsV, VC_B), (rsC, SC_B), (rsT, TM_B), (rsY, SY_B), (rsQ, QSC)]:
                            v.reg_mov(r, s)
                            v.reg_mul(r, r, k)
                        wrg(v, rt, "tM", rsT, None, bTM, clamp=None)
                        def ve_inner3(v):
                            with nc.Fori(0, H, step=2, engines=[ET.DVE]) as h2_:
                                h2 = urv(h2_, H - 2)
                                for par in range(2):
                                    pT = par * TM_H; pC = par * SC_H; p5 = par * 5
                                    wrg(v, rt, "tM", rsT, (h2, TM_H), bTM + pT + 2, clamp=None)
                                    v.tensor_scalar(nbb[par][:, 0:512], ps[4][:, :], -1.0, None,
                                                    op0=ALU.mult).then_inc(sem["vC"], 1)
                                    for kc in range(NKC):
                                        wrg(v, rt, "tM", rsT, (h2, TM_H), bTM + pT + sco_pos(kc), clamp=None)
                                        wrg(v, rt, "sC", rsC, (h2, SC_H), bSC + pC + kc + 2, clamp=None)
                                        wrg(v, rt, "vC", rsV, (h2, 5), bVC + p5 + 1, clamp=None)
                                        v.scalar_tensor_tensor(
                                            tmpf[kc % 2][:, 0:512], in0=ps[kc % 2][:, :], scalar=SCALE,
                                            in1=nbb[par][:, 0:512], op0=ALU.mult,
                                            op1=ALU.add).then_inc(sem["sY"], 1)
                                    wrg(v, rt, "tM", rsT, (h2, TM_H), bTM + pT + TM_H - 1, clamp=None)
                                    lb_ = psl if par == 0 else ps[5]
                                    v.tensor_scalar(brow, lb_[0:1, 0:512], 1e-30, None,
                                                    op0=ALU.add).then_inc(sem["vC"], 1)
                                    wrg(v, rt, "vC", rsV, (h2, 5), bVC + p5 + 2, clamp=None)
                                    v.reciprocal(brow, brow).then_inc(sem["vC"], 1)
                                    wrg(v, rt, "tM", rsT, (h2, TM_H), bTM + pT + TM_H, clamp=None)
                                    v.tensor_copy(rlb[par][:, 0:512], ps[4][:, :]).then_inc(sem["vC"], 1)
                                    wrg(v, rt, "vC", rsV, (h2, 5), bVC + p5 + 4, clamp=None)
                                    v.tensor_tensor(attn_bTp[par][:, bass.ts(h2, 512)], ps[2 + par][:, :],
                                                    rlb[par][:, 0:512], op=ALU.mult).then_inc(sem["vC"], 1)
                            free_snap(h2_)
                        ve_inner3(v)
                        for dq in range(4):
                            for t in range(4):
                                gidx = dq * 4 + t
                                posx = POS_X2[gidx]
                                subx = "dG" + str((baseG + posx) % 2)
                                wrg(v, rt, "tM", rsT, None, bTM + 16 * TM_H + gidx + 1, clamp=None)
                                wrg(v, rt, subx, rsQ, None, 16 * (bG2 + posx // 2 + 1), clamp=None)
                                if gidx >= 2:
                                    pprev = POS_ST[gidx - 2]
                                    off16 = (baseG + pprev) // 2 + 1 - bG2
                                else:
                                    pprev = POS_ST[gidx + 14]
                                    off16 = (baseG + pprev) // 2 + 1 - bG2 - DG_B // 2
                                subst = "dG" + str((baseG + pprev) % 2)
                                wrg(v, rt, subst, rsQ, None, 16 * (bG2 + off16), clamp=0)
                                v.tensor_tensor(x3st[gidx % 2][:, 0:512],
                                                ps[gidx % 2][:, :],
                                                x2sl[gidx % 2][:, 0:512],
                                                op=ALU.add).then_inc(sem["vC"], 1)
                    free_snap(s_)
                ew("vector", b_ve)

                def b_te(t_):
                    rt = wregr(t_, "te_rt")
                    rsV = wregr(t_, "te_rsV"); rsC = wregr(t_, "te_rsC")
                    rsY = wregr(t_, "te_rsY"); rsQ = wregr(t_, "te_rsQ")
                    with nc.Fori(0, NS, engines=[ET.PE]) as s_:
                        s = urv(s_, NS - 1)
                        for r, k in [(rsV, VC_B), (rsC, SC_B), (rsY, SY_B), (rsQ, QSC)]:
                            t_.reg_mov(r, s)
                            t_.reg_mul(r, r, k)
                        def te_inner3(t_):
                            with nc.Fori(0, H, step=2, engines=[ET.PE]) as h2_:
                                h2 = urv(h2_, H - 2)
                                for par in range(2):
                                    pC = par * SC_H; p5 = par * 5; pY = par * NKC
                                    sub = "dG" + str((baseG + par) % 2)
                                    wrg(t_, rt, "sC", rsC, (h2, SC_H), bSC + pC + 1, clamp=None)
                                    wrg(t_, rt, "vC", rsV, (h2, 5), bVC + p5 - 1, clamp=bVC)
                                    t_.matmul(ps[4][0:1, 0:512], onescol[:, 0:1], q2b[par][:, 0:512],
                                              start=True, stop=True).then_inc(sem["tM"], 1)
                                    wrg(t_, rt, "sC", rsC, (h2, SC_H), bSC + pC + 3, clamp=None)
                                    t_.matmul(ps[4][:, :], onesrow[0:1, :], brow,
                                              start=True, stop=True).then_inc(sem["tM"], 1)
                                    lb_ = psl if par == 0 else ps[5]
                                    wrg(t_, rt, sub, rsQ, (h2, 8), 16 * bG2 + 16, clamp=None)
                                    for kc in range(NKC + 2):
                                        if kc < NKC:
                                            wrg(t_, rt, "sY", rsY, (h2, NKC), bSY + pY + kc - 1, clamp=bSY)
                                            t_.matmul(ps[kc % 2][:, :], kTb[:, kc * 128:(kc + 1) * 128],
                                                      qTh[par][:, 0:512], start=True,
                                                      stop=True).then_inc(sem["tM"], 1)
                                        if kc >= 2:
                                            kp = kc - 2
                                            wrg(t_, rt, "sC", rsC, (h2, SC_H), bSC + pC + 4 + kp, clamp=None)
                                            t_.matmul(lb_[0:1, 0:512], onescolb[:, 0:1], pTb[kp % 2][:, 0:512],
                                                      start=(kp == 0), stop=(kp == NKC - 1))
                                            t_.matmul(ps[2 + par][:, :], vbuf[:, kp * 128:(kp + 1) * 128],
                                                      pTb[kp % 2][:, 0:512], start=(kp == 0),
                                                      stop=(kp == NKC - 1)).then_inc(sem["tM"], 1)
                                    wrg(t_, rt, "vC", rsV, (h2, 5), bVC + p5 + 3, clamp=None)
                                    t_.matmul(ps[4][:, :], onesrow[0:1, :], brow,
                                              start=True, stop=True).then_inc(sem["tM"], 1)
                            free_snap(h2_)
                        te_inner3(t_)
                        wrg(t_, rt, "vC", rsV, None, bVC + 80, clamp=None)
                        for dq in range(4):
                            sub = "dG" + str((baseG + POS_WO[dq]) % 2)
                            wrg(t_, rt, sub, rsQ, None, 16 * (bG2 + POS_WO[dq] // 2 + 1), clamp=None)
                            for t in range(4):
                                gidx = dq * 4 + t
                                wrg(t_, rt, "vC", rsV, None, bVC + 80 + gidx - 1, clamp=bVC)
                                t_.matmul(ps[gidx % 2][:, :], onesrow[0:1, :],
                                          BOROW[0:1, dq * 512:(dq + 1) * 512], start=True, stop=False)
                                for hch in range(16):
                                    ins = t_.matmul(ps[gidx % 2][:, :],
                                                    attn_bT[:, hch * 512 + t * 128: hch * 512 + t * 128 + 128],
                                                    wobuf[dq % 2][:, hch * 512:(hch + 1) * 512],
                                                    start=False, stop=(hch == 15))
                                ins.then_inc(sem["tM"], 1)
                    free_snap(s_)
                ew("tensor", b_te)

                add("vC", NS * VC_B); add("sC", NS * SC_B); add("tM", NS * TM_B)
                add("sY", NS * SY_B)
                marks["last_texM"] = get("tM")
                marks[("ffn_final", id(dst))] = [
                    ("dG0", 16 * ((Q["dG"] + 1) // 2)), ("dG1", 16 * (Q["dG"] // 2))]

            _p3b()

        emit_startup()
        KDRV = os.environ.get("KDRV", "")
        if KDRV:
            for spec in KDRV.split(","):
                engn, n = spec.split(":")
                n = int(n)
                def mk(engn=engn, n=n):
                    def fn(e):
                        for _ in range(n):
                            e.nop().then_inc(sem["vS" if engn in ("ve",) else "sS" if engn == "sc" else "tM" if engn == "te" else "sBC"], 1)
                    return fn
                ew({"ve": "vector", "sc": "scalar", "te": "tensor", "gp": "gpsimd"}[engn], mk())
                add({"ve": "vS", "sc": "sS", "te": "tM", "gp": "sBC"}[engn], n)
        cur = inp["x"]
        order = [p for p in ("p1", "p2", "p3", "p4") if p in phases]
        for pi, ph in enumerate(order):
            last = (pi == len(order) - 1)
            if ph == "p1":
                dst = y_out if last else xs[1]
                emit_attna(cur, dst)
                cur = dst
            elif ph == "p2":
                dst = y_out if last else xs[2]
                emit_ffn(cur, dst)
                cur = dst
            elif ph == "p4":
                emit_ffn(cur, y_out)
                cur = y_out
            elif ph == "p3":
                dst = y_out if last else xs[3]
                emit_attnb(cur, dst)
                cur = dst
            elif ph == "conv":
                pass
            else:
                raise NotImplementedError(ph)

        fin_marks = [("dG0", 16 * ((Q["dG"] + 1) // 2)), ("dG1", 16 * (Q["dG"] // 2))]
        def syf(g, fin_marks=fin_marks):
            for ssub, sval in fin_marks:
                g.wait_ge(sem[ssub], sval)
        ew("gpsimd", syf)

        block = en(nc.Block())

        @block.sync
        def _(s):
            for fn in prog["sync"]:
                fn(s)

        @block.vector
        def _(v):
            for fn in prog["vector"]:
                fn(v)

        @block.scalar
        def _(sc):
            for fn in prog["scalar"]:
                fn(sc)

        @block.tensor
        def _(t):
            for fn in prog["tensor"]:
                fn(t)

        @block.gpsimd
        def _(g):
            for fn in prog["gpsimd"]:
                fn(g)

    return nc


def make_consts(T):
    ROPE_DIM, THETA = 64, 10000.0
    inv = 1.0 / THETA ** (np.arange(0, ROPE_DIM, 2, dtype=np.float64) / ROPE_DIM)
    ang = np.arange(T, dtype=np.float64)[:, None] * inv[None, :]
    cos = np.cos(ang).astype(np.float32)
    sin = np.sin(ang).astype(np.float32)
    nb = T // M_BLK
    block_end = np.arange(256) * M_BLK + (M_BLK - 1)
    causal = block_end[None, :] < np.arange(T)[:, None]
    cmask = causal.astype(np.float32)
    cbias = np.where(causal, 0.0, NEG).astype(np.float32)
    diff = np.arange(2 * 256)[None, :] - np.arange(256)[:, None]
    win_ok = (diff > 0) & (diff <= 256)
    first_ok = np.arange(2 * 256) >= 256
    m = win_ok
    swm = np.where(m.T, 0.0, NEG).astype(np.float32)
    m0 = win_ok & first_ok[None, :]
    swm0 = np.where(m0.T, 0.0, NEG).astype(np.float32)
    return {
        "ident": np.eye(128, dtype=np.float32),
        "cosT": np.ascontiguousarray(cos.T), "sinT": np.ascontiguousarray(sin.T),
        "cos": cos, "sin": sin,
        "cmask": cmask, "cbias": cbias,
        "swm": swm, "swm0": swm0,
    }


def kernel(**inputs):
    T = 4096
    NCORES = 8
    nc = build_program(T=T, phases=("p1", "p2", "p3", "p4"))
    consts = make_consts(T)
    base = {}
    ren = {
        "w_qc": "L0_attn_a_w_qc", "w_qup": "L0_attn_a_w_qup",
        "kvc_w": "L0_attn_a_kvc_w", "kvc_wz": "L0_attn_a_kvc_wz",
        "kvc_bias": "L0_attn_a_kvc_bias", "k_proj_w": "L0_attn_a_k_proj_w",
        "v_proj_w": "L0_attn_a_v_proj_w", "idx_c_w": "L0_attn_a_idx_c_w",
        "idx_c_wz": "L0_attn_a_idx_c_wz", "idx_c_bias": "L0_attn_a_idx_c_bias",
        "w_dq": "L0_attn_a_w_dq", "w_iuq": "L0_attn_a_w_iuq", "w_w": "L0_attn_a_w_w",
        "q_norm_w": "L0_attn_a_q_norm_w", "k_norm_w": "L0_attn_a_k_norm_w",
        "group_w": "L0_attn_a_group_w", "final_w": "L0_attn_a_final_w",
        "Wq": "L0_attn_b_Wq", "bq": "L0_attn_b_bq", "Wk": "L0_attn_b_Wk",
        "bk": "L0_attn_b_bk", "Wv": "L0_attn_b_Wv", "bv": "L0_attn_b_bv",
        "Wo": "L0_attn_b_Wo", "bo": "L0_attn_b_bo",
        "wa": "L0_ffn_wa", "rg": "L0_ffn_rg", "rgb": "L0_ffn_rgb",
        "rw1": "L0_ffn_rw1", "rb1": "L0_ffn_rb1", "rw2": "L0_ffn_rw2",
        "rb2": "L0_ffn_rb2", "sg": "L0_ffn_sg", "sgb": "L0_ffn_sgb",
        "sw1": "L0_ffn_sw1", "sb1": "L0_ffn_sb1", "sw2": "L0_ffn_sw2",
        "sb2": "L0_ffn_sb2",
    }
    for k, rk in ren.items():
        base[k] = np.ascontiguousarray(inputs[rk], dtype=np.float32)
    base.update({k: v for k, v in consts.items() if k in (
        "ident", "cosT", "sinT", "cmask", "cbias", "swm", "swm0")})
    h = np.ascontiguousarray(inputs["L0_attn_a_h"], dtype=np.float32)
    in_maps = []
    for b in range(NCORES):
        m = dict(base)
        m["x"] = np.ascontiguousarray(h[b])
        in_maps.append(m)
    res = run_bass_kernel_spmd(nc, in_maps, core_ids=list(range(NCORES)))
    out = np.stack([res.results[b]["y"] for b in range(NCORES)], axis=0)
    return out.astype(np.float32)
